# version 01
# baseline (speedup 1.0000x reference)
import sys
sys.path.insert(0, '/opt/trn_rl_repo')
import contextlib
import os
import numpy as np
import concourse.bass as bass
import concourse.mybir as mybir

F32 = mybir.dt.float32
BF16 = mybir.dt.bfloat16
AF = mybir.ActivationFunctionType
ALU = mybir.AluOpType
AX = mybir.AxisListType

S, D = 4096, 2048
P = 2432
C = 128
NKH, NVH, DK, DV = 16, 32, 128, 128
KDIM, VDIM = NKH * DK, NVH * DV
NE = 8
T0, T1 = P, S - P
NC0, NC1 = T0 // C, T1 // C
LN_EPS, RMS_EPS, L2_EPS, FFN_EPS = 1e-5, 1e-6, 1e-6, 1e-6
CLAMP = 30.0
LNHALF_DK = -0.5 * float(np.log(128.0))
MAXPASS = 1280

ENGS = ("sync", "gpsimd", "tensor", "scalar", "vector")
BLANKET = ("gpsimd", "scalar", "vector")


def ts(i, n=128):
    return slice(i * n, (i + 1) * n)


class Script:
    def __init__(self, sems):
        self.items = {e: [] for e in ENGS}
        self.cnt = {}
        self.sems = sems

    def _bump(self, sem, amt):
        self.cnt[sem] = self.cnt.get(sem, 0) + amt
        return (sem, self.cnt[sem])

    def op(self, eng, emit, deps=(), dma=False):
        waits = {}
        for d in deps:
            if d is None:
                continue
            smn, v = d
            waits[smn] = max(waits.get(smn, 0), v)
        csem = eng + ("_d" if dma else "_c")
        if not dma and (eng in BLANKET or eng == "tensor"):
            waits.pop(csem, None)
        if not dma and eng in BLANKET:
            waits[csem] = self.cnt.get(csem, 0)
        wl = sorted(waits.items())
        h = self._bump(csem, 16 if dma else 1)
        self.items[eng].append((emit, wl, [(csem, 16 if dma else 1)]))
        return h

    def barrier_handle(self, eng, dma=False):
        csem = eng + ("_d" if dma else "_c")
        return (csem, self.cnt.get(csem, 0))

    def replay(self, engname, eng):
        for emit, waits, incs in self.items[engname]:
            for smn, v in waits:
                if v > 0:
                    eng.wait_ge(self.sems[smn], v)
            inst = emit(eng)
            for smn, amt in incs:
                inst.then_inc(self.sems[smn], amt)


def build(dbg=(), stage="all"):
    nc = bass.Bass("TRN2", target_bir_lowering=False, debug=False)

    xbt = nc.dram_tensor("xb", [S, D], F32, kind="ExternalInput").ap()
    w_qkvz = nc.dram_tensor("w_qkvz", [D, 12288], F32, kind="ExternalInput").ap()
    w_ba = nc.dram_tensor("w_ba", [D, 64], F32, kind="ExternalInput").ap()
    conv_w = nc.dram_tensor("conv_w", [2 * KDIM + VDIM, 4], F32, kind="ExternalInput").ap()
    dt_bias = nc.dram_tensor("dt_bias", [NVH, 1], F32, kind="ExternalInput").ap()
    a_log = nc.dram_tensor("a_log", [NVH, 1], F32, kind="ExternalInput").ap()
    anorm_w = nc.dram_tensor("anorm_w", [DV, 1], F32, kind="ExternalInput").ap()
    w_out = nc.dram_tensor("w_out", [VDIM, D], F32, kind="ExternalInput").ap()
    router_w = nc.dram_tensor("router_w", [D, NE], F32, kind="ExternalInput").ap()
    router_b = nc.dram_tensor("router_b", [1, NE], F32, kind="ExternalInput").ap()
    expert_w = nc.dram_tensor("expert_w", [NE * D, D], F32, kind="ExternalInput").ap()
    expert_b = nc.dram_tensor("expert_b", [NE, D], F32, kind="ExternalInput").ap()
    proj_w = nc.dram_tensor("proj_w", [D, D], F32, kind="ExternalInput").ap()
    proj_b = nc.dram_tensor("proj_b", [1, D], F32, kind="ExternalInput").ap()
    fnorm_w = nc.dram_tensor("fnorm_w", [1, D], F32, kind="ExternalInput").ap()
    out = nc.dram_tensor("out", [T0, D], F32, kind="ExternalOutput").ap()

    wq_cc = w_qkvz.rearrange("(kt p) (cc c) -> cc p kt c", p=128, c=128)
    wba_v = w_ba.rearrange("(kt p) c -> p kt c", p=128)
    cw_cc = conv_w.rearrange("(cc p) j -> cc p j", p=128)
    we_v = expert_w.rearrange("(ekt p) c -> ekt p c", p=128)
    wo_v = w_out.rearrange("(kt p) c -> kt p c", p=128)
    pw_v = proj_w.rearrange("(kt p) c -> kt p c", p=128)
    rw_v = router_w.rearrange("(kt p) c -> p kt c", p=128)

    qs = nc.dram_tensor("qs", [KDIM, S], BF16, kind="Internal").ap()
    kfs = nc.dram_tensor("kfs", [KDIM, S], BF16, kind="Internal").ap()
    kts_d = nc.dram_tensor("kts", [S, KDIM], BF16, kind="Internal").ap()
    vts_d = nc.dram_tensor("vts", [S, VDIM], BF16, kind="Internal").ap()
    zs = nc.dram_tensor("zs", [VDIM, S], BF16, kind="Internal").ap()
    bas = nc.dram_tensor("bas", [64, S], F32, kind="Internal").ap()
    ys = nc.dram_tensor("ys", [VDIM, S], BF16, kind="Internal").ap()
    x1s = nc.dram_tensor("x1s", [T0, D], F32, kind="Internal").ap()
    ln2s = nc.dram_tensor("ln2s", [T0, D], BF16, kind="Internal").ap()
    l2ts = nc.dram_tensor("l2ts", [D, T0], BF16, kind="Internal").ap()
    gates = nc.dram_tensor("gates", [T0, NE], F32, kind="Internal").ap()
    rfl = nc.dram_tensor("rfl", [224, C], F32, kind="Internal").ap()

    qs_v = qs.rearrange("(h p) s -> p h s", p=128)
    kfs_v = kfs.rearrange("(h p) s -> p h s", p=128)
    zs_v = zs.rearrange("(h p) s -> p h s", p=128)
    ys_v = ys.rearrange("(h p) s -> p h s", p=128)
    kts_t = kts_d.rearrange("(n p) c -> n p c", p=128)
    vts_t = vts_d.rearrange("(n p) c -> n p c", p=128)
    bas_hf = bas.rearrange("(h f) s -> h f s", f=4)
    l2ts_v = l2ts.rearrange("(kt p) s -> p kt s", p=128)
    we_v2 = expert_w.rearrange("(e h j p) c -> e h p j c", h=2, j=8, p=128)
    pw_v2 = proj_w.rearrange("(h j p) c -> h p j c", h=2, j=8, p=128)
    wo_v2 = w_out.rearrange("(kt p) c -> kt p c", p=128)

    dbg_outs = {}
    for name, shape in dbg:
        dbg_outs[name] = nc.dram_tensor("dbg_" + name, list(shape), F32,
                                        kind="ExternalOutput").ap()

    with contextlib.ExitStack() as ctx:
        E = ctx.enter_context

        sb_ident = E(nc.sbuf_tensor("sb_ident", [128, 128], BF16))
        sb_ones1 = E(nc.sbuf_tensor("sb_ones1", [128, 1], BF16))
        sb_onesr = E(nc.sbuf_tensor("sb_onesr", [1, 128], BF16))
        sb_onesrf = E(nc.sbuf_tensor("sb_onesrf", [1, 128], F32))
        sb_masks = E(nc.sbuf_tensor("sb_masks", [128, 3, 128], BF16))
        sb_identf = E(nc.sbuf_tensor("sb_identf", [128, 128], F32))
        sb_iota = E(nc.sbuf_tensor("sb_iota", [128, 128], F32))
        sb_alog = E(nc.sbuf_tensor("sb_alog", [32, 1], F32))
        sb_dtb = E(nc.sbuf_tensor("sb_dtb", [32, 1], F32))
        sb_negea = E(nc.sbuf_tensor("sb_negea", [32, 1], F32))
        sb_anwc = E(nc.sbuf_tensor("sb_anwc", [128, 1], F32))
        sb_z32 = E(nc.sbuf_tensor("sb_z32", [32, 128], F32))
        sb_eps = E(nc.sbuf_tensor("sb_eps", [128, 4], F32))
        sb_S = E(nc.sbuf_tensor("sb_S", [128, NVH, DV], F32))
        sb_Sb = E(nc.sbuf_tensor("sb_Sb", [128, NVH, DV], BF16))

        AR = 168 * 1024
        sb_ar = E(nc.sbuf_tensor("sb_ar", [128, AR], mybir.dt.uint8))

        def av(off, dtype, shape):
            esz = mybir.dt.size(dtype)
            n = int(np.prod(shape))
            assert off % 4 == 0 and off + n * esz <= AR, (off, n * esz, AR)
            v = sb_ar[:, off:off + n * esz].bitcast(dtype)
            if len(shape) > 1:
                names = " ".join(f"d{i}" for i in range(len(shape)))
                v = v.rearrange(f"p ({names}) -> p {names}",
                                **{f"d{i}": shape[i] for i in range(len(shape))})
            return v

        ps = E(nc.psum_tensor("ps", [128, 8, 512], F32))
        psb = ps.bitcast(BF16)

        semnames = [e + "_c" for e in ENGS] + [e + "_d" for e in ("sync", "gpsimd", "scalar")]
        sems = {n: E(nc.semaphore(n)) for n in semnames}
        block = E(nc.Block())

        _o = [0]

        def alloc(n):
            r = (_o[0] + 3) & ~3
            _o[0] = r + n
            assert _o[0] <= AR, _o[0]
            return r

        A_x = alloc(2 * D * 4)
        A_ln = alloc(2 * D * 2)
        A_st = alloc(2 * 8 * 4)
        A_dump = alloc(D * 4)
        A_lnxT = alloc(16 * (MAXPASS + 128) * 2)
        F_w = alloc(2 * 16 * 128 * 2)
        F_cw = alloc(2 * 4 * 4)
        F_proj = alloc(2 * (MAXPASS + 3) * 2)
        F_conv = alloc(2 * MAXPASS * 4)
        F_post = alloc(2 * MAXPASS * 2)
        F_tt = alloc(10 * 128 * 2)
        F_ba = alloc(MAXPASS * 4)
        FRONT_END = _o[0]

        v_x = av(A_x, F32, [2, D])
        v_ln = av(A_ln, BF16, [2, D])
        v_st = av(A_st, F32, [2, 8])
        v_dump = av(A_dump, F32, [D])
        v_lnxT = av(A_lnxT, BF16, [16, MAXPASS + 128])
        v_w = av(F_w, BF16, [2, 16, 128])
        v_cw = av(F_cw, F32, [2, 4])
        v_proj = av(F_proj, BF16, [2, MAXPASS + 3])
        v_conv = av(F_conv, F32, [2, MAXPASS])
        v_post = av(F_post, BF16, [2, MAXPASS])
        v_tt = av(F_tt, BF16, [10, 128])
        v_barow = av(F_ba, F32, [MAXPASS])

        _o[0] = 0
        SCq = alloc(NKH * C * 2)
        SCk = alloc(NKH * C * 2)
        SCkt = alloc(KDIM * 2)
        SCvt = alloc(VDIM * 2)
        SCz = alloc(NVH * C * 2)
        SCba = alloc(2 * C * 4)
        SCrows = alloc(14 * C * 4)
        SCsq2 = alloc(2048 * 2)
        SCsqr = alloc(2 * C * 4)
        SCAT = alloc(16 * C * 2)
        SCB0 = alloc(16 * C * 2)
        SCET = alloc(16 * C * 2)
        SCKKs = alloc(NKH * C * 2)
        SCKKl = alloc(NKH * C * 2)
        SCKQ = alloc(NKH * C * 2)
        SCF2 = alloc(16 * C * 4)
        SCARG = alloc(16 * C * 4)
        SCdupb = alloc(16 * C * 2)
        SCPT = alloc(3 * 32 * 4)
        SCpw = alloc(6 * 16 * C * 2)
        SCpb = alloc(2 * 16 * C * 2)
        SCX = alloc(16 * C * 2)
        SCkp = alloc(16 * C * 2)
        SCqp = alloc(16 * C * 2)
        SCkpp = alloc(16 * DK * 2)
        SCzsil = alloc(16 * C * 2)
        SCxg = alloc(16 * C * 2)
        SCgsq = alloc(16 * C * 2)
        SCy = alloc(16 * C * 2)
        SCrow1 = alloc(2048 * 4)
        SCrowb = alloc(2048 * 2)
        SClbc = alloc(32 * 4)
        SCAN_END = _o[0]

        w_q = av(SCq, BF16, [NKH, C])
        w_k = av(SCk, BF16, [NKH, C])
        w_kt = av(SCkt, BF16, [KDIM])
        w_vt = av(SCvt, BF16, [VDIM])
        w_z = av(SCz, BF16, [NVH, C])
        w_ba = av(SCba, F32, [2, C])
        w_rows = av(SCrows, F32, [14, C])
        w_sq2 = av(SCsq2, BF16, [2048])
        w_sqr = av(SCsqr, F32, [2, C])
        w_AT = av(SCAT, BF16, [16 * C])
        w_B0 = av(SCB0, BF16, [16 * C])
        w_ET = av(SCET, BF16, [16 * C])
        w_KKs = av(SCKKs, BF16, [NKH, C])
        w_KKl = av(SCKKl, BF16, [NKH, C])
        w_KQ = av(SCKQ, BF16, [NKH, C])
        w_F2 = av(SCF2, F32, [16 * C])
        w_ARG = av(SCARG, F32, [16 * C])
        w_dupb = av(SCdupb, BF16, [16 * C])
        w_PT = av(SCPT, F32, [3, 32])
        w_pw = av(SCpw, BF16, [6, 16 * C])
        w_pb = av(SCpb, BF16, [2, 16 * C])
        w_X = av(SCX, BF16, [16 * C])
        w_kp = av(SCkp, BF16, [16 * C])
        w_qp = av(SCqp, BF16, [16 * C])
        w_kpp = av(SCkpp, BF16, [16 * DK])
        w_zsil = av(SCzsil, BF16, [16 * C])
        w_xg = av(SCxg, BF16, [16 * C])
        w_gsq = av(SCgsq, BF16, [16 * C])
        w_y = av(SCy, BF16, [16 * C])
        w_row1 = av(SCrow1, F32, [2048])
        w_rowb = av(SCrowb, BF16, [2048])
        w_lbc = av(SClbc, F32, [32])

        _o[0] = 0
        E1wo = alloc(32 * 2048 * 2)
        E1ys = alloc(32 * 128 * 2)
        E1xb = alloc(2048 * 4)
        E1x1 = alloc(2048 * 4)
        E1l2b = alloc(2048 * 2)
        E1l2T = alloc(16 * 128 * 2)
        E1rw = alloc(16 * 8 * 2)
        E1rb = alloc(8 * 2)
        E1lg = alloc(8 * 4)
        E1t8a = alloc(8 * 4)
        E1t8b = alloc(8 * 4)
        E1st = alloc(8 * 4)
        assert _o[0] <= AR

        e1_wo = av(E1wo, BF16, [32, 2048])
        e1_ys = av(E1ys, BF16, [32, 128])
        e1_sq2 = av(E1ys, F32, [2048])
        e1_xb = av(E1xb, F32, [2048])
        e1_x1 = av(E1x1, F32, [2048])
        e1_l2b = av(E1l2b, BF16, [2048])
        e1_l2T = av(E1l2T, BF16, [16, 128])
        e1_rw = av(E1rw, BF16, [16, 8])
        e1_rb = av(E1rb, BF16, [8])
        e1_lg = av(E1lg, F32, [8])
        e1_t8a = av(E1t8a, F32, [8])
        e1_t8b = av(E1t8b, F32, [8])
        e1_st = av(E1st, F32, [8])

        GSG = 7
        _o[0] = 0
        E2w = alloc(2 * 8 * 2048 * 2)
        E2l2T = alloc(GSG * 16 * 128 * 2)
        E2comb = alloc(GSG * 2048 * 2)
        E2h = alloc(2048 * 4)
        E2cT = alloc(16 * 128 * 2)
        E2x = alloc(2048 * 2)
        E2x1 = alloc(2048 * 4)
        E2fr = alloc(2048 * 4)
        E2fnb = alloc(2048 * 4)
        E2eb = alloc(2048 * 2)
        E2gt = alloc(GSG * 8 * 4)
        E2st = alloc(8 * 4)
        assert _o[0] <= AR

        e2_w = av(E2w, BF16, [2, 8, 2048])
        e2_l2T = av(E2l2T, BF16, [GSG, 16, 128])
        e2_comb = av(E2comb, BF16, [GSG, 2048])
        e2_h = av(E2h, BF16, [2048])
        e2_y = av(E2h, F32, [2048])
        e2_cT = av(E2cT, BF16, [16, 128])
        e2_x = av(E2x, BF16, [2048])
        e2_x1 = av(E2x1, F32, [2048])
        e2_fr = av(E2fr, F32, [2048])
        e2_fnb = av(E2fnb, F32, [2048])
        e2_eb = av(E2eb, BF16, [2048])
        e2_gt = av(E2gt, F32, [GSG, 8])
        e2_st = av(E2st, F32, [8])

        RFL_KSQ, RFL_QSQ, RFL_PA, RFL_FA, RFL_FE, RFL_ESA, RFL_EQE, RFL_LC = \
            0, 16, 32, 64, 96, 128, 160, 192

        def build_script(role):
            sc = Script(sems)
            passes = ([(0, 1280, True), (1280, 1152, True)] if role == 0 else
                      [(0, 1280, False), (1280, 1152, False),
                       (2432, 1280, True), (3712, 384, True)])

            last_iota_reader = [None]

            def iota_mask(dst, opk):
                hi = sc.op("gpsimd", lambda e: e.iota(
                    sb_iota[:, :], [[1, 128]], base=0, channel_multiplier=-1,
                    allow_small_or_imprecise_dtypes=True), deps=[last_iota_reader[0]])
                sc.op("vector", lambda e, opk=opk: e.tensor_single_scalar(
                    sb_iota[:, :], sb_iota[:, :], 0.0, opk), deps=[hi])
                hd = sc.op("vector", lambda e, dst=dst: e.tensor_copy(dst, sb_iota[:, :]))
                last_iota_reader[0] = hd
                return hd

            iota_mask(sb_ident[:, :], ALU.is_equal)
            iota_mask(sb_masks[:, 0, :], ALU.is_gt)
            iota_mask(sb_masks[:, 1, :], ALU.is_ge)
            iota_mask(sb_masks[:, 2, :], ALU.is_lt)
            sc.op("vector", lambda e: e.memset(sb_ones1[:, :], 1.0))
            sc.op("vector", lambda e: e.memset(sb_onesr[:, :], 1.0))
            sc.op("vector", lambda e: e.memset(sb_onesrf[:, :], 1.0))
            sc.op("vector", lambda e: e.memset(sb_z32[:, :], 0.0))
            sc.op("vector", lambda e: e.memset(sb_eps[:, 0:1], LN_EPS))
            sc.op("vector", lambda e: e.memset(sb_eps[:, 1:2], 1e-6))
            sc.op("vector", lambda e: e.memset(sb_eps[:, 2:3], 1.0))
            sc.op("vector", lambda e: e.memset(sb_S[:, :, :], 0.0))
            sc.op("vector", lambda e: e.memset(sb_Sb[:, :, :], 0.0))
            hi2 = sc.op("gpsimd", lambda e: e.iota(
                sb_iota[:, :], [[1, 128]], base=0, channel_multiplier=-1,
                allow_small_or_imprecise_dtypes=True), deps=[last_iota_reader[0]])
            sc.op("vector", lambda e: e.tensor_single_scalar(
                sb_identf[:, :], sb_iota[:, :], 0.0, ALU.is_equal), deps=[hi2])
            h1 = sc.op("sync", lambda e: e.dma_start(sb_alog[:, :], a_log), dma=True)
            sc.op("sync", lambda e: e.dma_start(sb_dtb[:, :], dt_bias), dma=True)
            sc.op("sync", lambda e: e.dma_start(sb_anwc[:, :], anorm_w), dma=True)
            h4 = sc.op("scalar", lambda e: e.activation(sb_negea[:, :], sb_alog[:, :], AF.Exp),
                       deps=[h1])
            sc.op("vector", lambda e: e.tensor_scalar_mul(sb_negea[:, :], sb_negea[:, :], -1.0),
                  deps=[h4])
            consts_done = {e: sc.barrier_handle(e) for e in ENGS}

            def phaseA(tstart, ntok, lnxT_free_h, prev_pass_banks_h, pfs):
                astart = max(tstart - 128, 0)
                atiles = (tstart + ntok - astart) // 128
                hload, hnorm, htr, hev = {}, {}, {}, {}
                last_ev = None
                for m in range(atiles):
                    b = m % 2
                    row = astart // 128 + m
                    xdeps = ([hnorm.get(m - 2)] if m >= 2 else pfs.get("norm", []))
                    hload[m] = sc.op("sync", lambda e, b=b, row=row: e.dma_start(
                        v_x[:, b, :], xbt[ts(row), :]),
                        deps=xdeps + [consts_done["vector"]], dma=True)
                    hsq = sc.op("scalar", lambda e, b=b: e.activation(
                        v_dump[:, :], v_x[:, b, :], AF.Square, accum_out=v_st[:, b, 1:2]),
                        deps=[hload[m]])
                    sc.op("vector", lambda e, b=b: e.tensor_reduce(
                        v_st[:, b, 0:1], v_x[:, b, :], AX.X, ALU.add), deps=[hload[m]])
                    sc.op("vector", lambda e, b=b: e.tensor_scalar_mul(
                        v_st[:, b, 2:3], v_st[:, b, 0:1], 1.0 / D))
                    sc.op("vector", lambda e, b=b: e.tensor_scalar(
                        v_st[:, b, 4:5], v_st[:, b, 2:3], v_st[:, b, 2:3], None, ALU.mult))
                    sc.op("vector", lambda e, b=b: e.tensor_scalar(
                        v_st[:, b, 5:6], v_st[:, b, 1:2], 1.0 / D, None, ALU.mult), deps=[hsq])
                    hvar = sc.op("vector", lambda e, b=b: e.tensor_tensor(
                        v_st[:, b, 5:6], v_st[:, b, 5:6], v_st[:, b, 4:5], ALU.subtract))
                    sc.op("scalar", lambda e, b=b: e.activation(
                        v_st[:, b, 6:7], v_st[:, b, 5:6], AF.Ln, bias=sb_eps[:, 0:1]),
                        deps=[hvar])
                    hrs = sc.op("scalar", lambda e, b=b: e.activation(
                        v_st[:, b, 7:8], v_st[:, b, 6:7], AF.Exp, scale=-0.5))
                    ndeps = ([htr.get((m - 2, 15))] if m >= 2 else pfs.get("lasttr", []))
                    hnorm[m] = sc.op("vector", lambda e, b=b: e.tensor_scalar(
                        v_ln[:, b, :], v_x[:, b, :], v_st[:, b, 2:3], v_st[:, b, 7:8],
                        ALU.subtract, ALU.mult),
                        deps=[hrs] + ndeps)
                    for kt in range(16):
                        g = m * 16 + kt
                        tb = g % 2
                        htr[(m, kt)] = sc.op("tensor", lambda e, b=b, kt=kt, tb=tb: e.transpose(
                            psb[:, 6 + tb, 0:128], v_ln[:, b, ts(kt)], sb_ident[:, :]),
                            deps=[hnorm[m], hev.get(g - 2), consts_done["vector"],
                                  (prev_pass_banks_h if g < 2 else None)])
                        hev[g] = sc.op("scalar", lambda e, kt=kt, tb=tb, m=m: e.copy(
                            v_lnxT[:, kt, ts(m)], psb[:, 6 + tb, 0:128]),
                            deps=[htr[(m, kt)], lnxT_free_h])
                        last_ev = hev[g]
                phA_out = dict(norm=[hnorm[atiles - 1], hnorm.get(atiles - 2)],
                               lasttr=[htr[(atiles - 1, 15)], htr.get((atiles - 2, 15))])
                return astart, atiles, last_ev, phA_out

            def front_pass(tstart, ntok, full, lnxT_free_h, prev_pass_banks_h, pfs):
                astart, atiles, lnxT_ready, phA_out = phaseA(tstart, ntok, lnxT_free_h,
                                                              prev_pass_banks_h, pfs)
                halo = 3 if tstart > 0 else 0
                off = (tstart - astart) - halo
                ncols = ntok + halo
                nsl = (ncols + 511) // 512
                ntile = ntok // 128

                fam = []
                if full:
                    for h in range(16):
                        fam.append((6 * h, h, qs_v[:, h, tstart:tstart + ntok], None))
                for h in range(16):
                    fam.append((6 * h + 1, 16 + h, kfs_v[:, h, tstart:tstart + ntok], ("k", h)))
                for i in range(32):
                    fam.append((6 * (i // 2) + 2 + (i % 2), 32 + i, None, ("v", i)))
                if full:
                    for i in range(32):
                        fam.append((6 * (i // 2) + 4 + (i % 2), None,
                                    zs_v[:, i, tstart:tstart + ntok], None))

                hW, hmm, hpe, hsl, hcons = {}, {}, {}, {}, {}
                htr_last_store = pfs.get("trstore")
                last_tr_evac = None
                for cidx, (colc, convrow, fmdest, trspec) in enumerate(fam):
                    b = cidx % 2
                    has_conv = convrow is not None
                    wdeps = ([hmm.get(cidx - 2)] if cidx >= 2 else pfs.get("mms", []))
                    sldeps = ([hsl.get(cidx - 2)] if cidx >= 2 else pfs.get("sl", []))
                    hW[cidx] = sc.op("gpsimd", lambda e, b=b, colc=colc: e.dma_start(
                        v_w[:, b, :, :], wq_cc[colc]),
                        deps=wdeps, dma=True)
                    hcw = None
                    if has_conv:
                        hcw = sc.op("gpsimd", lambda e, b=b, cr=convrow: e.dma_start(
                            v_cw[:, b, :], cw_cc[cr]),
                            deps=sldeps, dma=True)

                    def mms(e, b=b, nsl=nsl, ncols=ncols, off=off):
                        last = None
                        for kt in range(16):
                            for sl in range(nsl):
                                c0 = sl * 512
                                cn = min(512, ncols - c0)
                                last = e.matmul(ps[:, b * 3 + sl, 0:cn], v_w[:, b, kt, :],
                                                v_lnxT[:, kt, off + c0: off + c0 + cn],
                                                start=(kt == 0), stop=(kt == 15))
                        return last
                    hmm[cidx] = sc.op("tensor", mms,
                                      deps=[hW[cidx], lnxT_ready, hpe.get(cidx - 2)])

                    def evac(e, b=b, nsl=nsl, ncols=ncols):
                        last = None
                        for sl in range(nsl):
                            c0 = sl * 512
                            cn = min(512, ncols - c0)
                            last = e.copy(v_proj[:, b, c0:c0 + cn], ps[:, b * 3 + sl, 0:cn])
                        return last
                    hpe[cidx] = sc.op("scalar", evac, deps=[hmm[cidx]] + sldeps)

                    if has_conv:
                        hcv = None
                        for j in range(4):
                            sh = halo + j - 3
                            def cop(e, b=b, j=j, sh=sh, ntok=ntok):
                                if sh >= 0:
                                    if j == 0:
                                        return e.tensor_scalar(
                                            v_conv[:, b, 0:ntok], v_proj[:, b, sh:sh + ntok],
                                            v_cw[:, b, 0:1], None, ALU.mult)
                                    return e.scalar_tensor_tensor(
                                        v_conv[:, b, 0:ntok], v_proj[:, b, sh:sh + ntok],
                                        v_cw[:, b, j:j + 1], v_conv[:, b, 0:ntok],
                                        ALU.mult, ALU.add)
                                t0j = -sh
                                if j == 0:
                                    e.memset(v_conv[:, b, 0:t0j], 0.0)
                                    return e.tensor_scalar(
                                        v_conv[:, b, t0j:ntok], v_proj[:, b, 0:ntok - t0j],
                                        v_cw[:, b, 0:1], None, ALU.mult)
                                return e.scalar_tensor_tensor(
                                    v_conv[:, b, t0j:ntok], v_proj[:, b, 0:ntok - t0j],
                                    v_cw[:, b, j:j + 1], v_conv[:, b, t0j:ntok],
                                    ALU.mult, ALU.add)
                            hcv = sc.op("vector", cop,
                                        deps=[hpe[cidx], hcw] + sldeps)
                        pcons = (hcons.get(cidx - 2, []) if cidx >= 2
                                 else pfs.get("cons", []))
                        hsl[cidx] = sc.op("scalar", lambda e, b=b, ntok=ntok: e.activation(
                            v_post[:, b, 0:ntok], v_conv[:, b, 0:ntok], AF.Silu),
                            deps=[hcv] + pcons)
                    else:
                        pcons = (hcons.get(cidx - 2, []) if cidx >= 2
                                 else pfs.get("cons", []))
                        hsl[cidx] = sc.op("scalar", lambda e, b=b, ntok=ntok, halo=halo: e.copy(
                            v_post[:, b, 0:ntok], v_proj[:, b, halo:halo + ntok]),
                            deps=[hpe[cidx]] + pcons)

                    consumers = []
                    if fmdest is not None:
                        consumers.append(sc.op("sync", lambda e, b=b, d=fmdest, ntok=ntok:
                                               e.dma_start(d, v_post[:, b, 0:ntok]),
                                               deps=[hsl[cidx]], dma=True))
                    if trspec is not None:
                        kind, hh = trspec
                        hte = {}
                        for tt in range(ntile):
                            g = tt % 2
                            htp = sc.op("tensor", lambda e, b=b, tt=tt, g=g: e.transpose(
                                psb[:, 6 + g, 0:128], v_post[:, b, ts(tt)], sb_ident[:, :]),
                                deps=[hsl[cidx], hte.get(tt - 2), htr_last_store,
                                      (lnxT_ready if tt < 2 else None)])
                            hte[tt] = sc.op("scalar", lambda e, tt=tt, g=g: e.copy(
                                v_tt[:, tt, :], psb[:, 6 + g, 0:128]), deps=[htp])
                        consumers.append(hte[ntile - 1])
                        dst = (kts_d if kind == "k" else vts_d)

                        def trstore(e, hh=hh, dst=dst, tstart=tstart, ntile=ntile):
                            dview = dst[tstart:tstart + ntile * 128, ts(hh)].rearrange(
                                "(n p) c -> p n c", p=128)
                            return e.dma_start(dview, v_tt[:, 0:ntile, :])
                        htr_last_store = sc.op("sync", trstore,
                                               deps=[hte[ntile - 1]], dma=True)
                        consumers.append(htr_last_store)
                    hcons[cidx] = consumers

                nf = len(fam)
                hWba = sc.op("gpsimd", lambda e: e.dma_start(v_w[:, 0, :, 0:64], wba_v),
                             deps=[hmm.get(nf - 2)], dma=True)

                def bamms(e, ntok=ntok, off=off, halo=halo):
                    last = None
                    nsl2 = (ntok + 511) // 512
                    for kt in range(16):
                        for sl in range(nsl2):
                            c0 = sl * 512
                            cn = min(512, ntok - c0)
                            last = e.matmul(ps[:, sl, 0:cn][0:64, :], v_w[:, 0, kt, 0:64],
                                            v_lnxT[:, kt, off + halo + c0: off + halo + c0 + cn],
                                            start=(kt == 0), stop=(kt == 15))
                    return last
                hbamm = sc.op("tensor", bamms,
                              deps=[hWba, lnxT_ready, hpe.get(nf - 1), hpe.get(nf - 2)])

                def baevac(e, ntok=ntok):
                    last = None
                    nsl2 = (ntok + 511) // 512
                    for sl in range(nsl2):
                        c0 = sl * 512
                        cn = min(512, ntok - c0)
                        last = e.copy(v_barow[0:64, c0:c0 + cn], ps[:, sl, 0:cn][0:64, :])
                    return last
                hbae = sc.op("scalar", baevac, deps=[hbamm, pfs.get("bast")])
                hbast = sc.op("sync", lambda e, tstart=tstart, ntok=ntok: e.dma_start(
                    bas[:, tstart:tstart + ntok], v_barow[0:64, 0:ntok]),
                    deps=[hbae], dma=True)
                nf2 = len(fam)
                fs = dict(mms=[hmm[nf2 - 1], hmm.get(nf2 - 2), hbamm],
                          sl=[hsl[nf2 - 1], hsl.get(nf2 - 2)],
                          cons=hcons.get(nf2 - 1, []) + hcons.get(nf2 - 2, []) +
                               [hbae],
                          bast=hbast, trstore=htr_last_store,
                          norm=phA_out["norm"], lasttr=phA_out["lasttr"])
                return hbamm, sc.barrier_handle("scalar"), fs

            def scan_chunk(sc, ch, outc, st):
                t0c = ch * C
                tsl = slice(t0c, t0c + C)
                hLk = sc.op("sync", lambda e, tsl=tsl: e.dma_start(
                    w_k[:, :, :], kfs_v[:, :, tsl]), deps=st["kcons"], dma=True)
                hLkt = sc.op("sync", lambda e, tsl=tsl: e.dma_start(
                    w_kt[:, :], kts_d[tsl, :]), deps=st["ktcons"], dma=True)
                hLvt = sc.op("sync", lambda e, tsl=tsl: e.dma_start(
                    w_vt[:, :], vts_d[tsl, :]), deps=st["vtcons"], dma=True)
                hLb = sc.op("sync", lambda e, tsl=tsl: e.dma_start(
                    w_ba[0:32, 0, :], bas_hf[:, 0:2, tsl]),
                    deps=st["bacons"], dma=True)
                hLa = sc.op("sync", lambda e, tsl=tsl: e.dma_start(
                    w_ba[0:32, 1, :], bas_hf[:, 2:4, tsl]),
                    dma=True)
                hLq = hLz = None
                if outc:
                    hLq = sc.op("sync", lambda e, tsl=tsl: e.dma_start(
                        w_q[:, :, :], qs_v[:, :, tsl]), deps=st["qcons"], dma=True)
                    hLz = sc.op("sync", lambda e, tsl=tsl: e.dma_start(
                        w_z[:, :, :], zs_v[:, :, tsl]), deps=st["zcons"], dma=True)

                R = lambda i: w_rows[0:32, i, :]
                rdeps = st["rowscons"]
                sc.op("vector", lambda e: e.tensor_scalar(R(10), w_ba[0:32, 1, :],
                      sb_dtb[0:32, :], None, ALU.add), deps=[hLa] + rdeps)
                sc.op("vector", lambda e: e.tensor_single_scalar(R(11), R(10), 0.0, ALU.max))
                sc.op("scalar", lambda e: e.activation(R(10), R(10), AF.Abs),
                      deps=[sc.barrier_handle("vector")] + rdeps)
                sc.op("scalar", lambda e: e.activation(R(10), R(10), AF.Exp, scale=-1.0))
                sc.op("scalar", lambda e: e.activation(R(10), R(10), AF.Ln,
                      bias=sb_eps[0:32, 2:3]))
                sc.op("vector", lambda e: e.tensor_tensor(R(10), R(10), R(11), ALU.add),
                      deps=[sc.barrier_handle("scalar")])
                sc.op("vector", lambda e: e.tensor_scalar(R(0), R(10), sb_negea[0:32, :],
                      None, ALU.mult))
                hG = sc.op("vector", lambda e: e.tensor_tensor_scan(
                    R(1), R(0), sb_z32[0:32, :], 0.0, ALU.add, ALU.add))
                hSg = sc.op("scalar", lambda e: e.activation(R(10), w_ba[0:32, 0, :],
                            AF.Sigmoid), deps=[hLb] + rdeps)
                hLnb = sc.op("scalar", lambda e: e.activation(R(2), R(10), AF.Ln))
                hKsq = sc.op("scalar", lambda e: e.activation(
                    w_sq2.rearrange("p (a b) -> p a b", a=16), w_k[:, :, :], AF.Square),
                    deps=[hLk, st["b7"]])
                hKrow = None
                row1_busy = st.get("row1cons")
                for sl in range(4):
                    hmmk = sc.op("tensor", lambda e, sl=sl: e.matmul(
                        ps[0:1, 7, 0:512], sb_ones1[:, :], w_sq2[:, ts(sl, 512)],
                        start=True, stop=True),
                        deps=[hKsq, st["b7"], hKrow])
                    hKrow = sc.op("vector", lambda e, sl=sl: e.tensor_copy(
                        w_row1[0:1, ts(sl, 512)], ps[0:1, 7, 0:512]),
                        deps=[hmmk, row1_busy])
                st["b7"] = hKrow
                hKst = sc.op("sync", lambda e: e.dma_start(
                    rfl[RFL_KSQ:RFL_KSQ + 16, :], w_row1[0:1, :]), deps=[hKrow], dma=True)
                hKld = sc.op("sync", lambda e: e.dma_start(
                    w_sqr[0:32, 0, :],
                    rfl[RFL_KSQ:RFL_KSQ + 16, :].unsqueeze(1).broadcast_to([16, 2, C])),
                    deps=[hKst], dma=True)
                hLnk = sc.op("scalar", lambda e: e.activation(
                    R(3), w_sqr[0:32, 0, :], AF.Ln, bias=sb_eps[0:32, 1:2]), deps=[hKld])
                hLnq = None
                if outc:
                    hQsq = sc.op("scalar", lambda e: e.activation(
                        w_sq2.rearrange("p (a b) -> p a b", a=16), w_q[:, :, :], AF.Square),
                        deps=[hLq, hKrow])
                    hQrow = None
                    for sl in range(4):
                        hmmq = sc.op("tensor", lambda e, sl=sl: e.matmul(
                            ps[0:1, 7, 0:512], sb_ones1[:, :], w_sq2[:, ts(sl, 512)],
                            start=True, stop=True),
                            deps=[hQsq, st["b7"], hQrow])
                        hQrow = sc.op("vector", lambda e, sl=sl: e.tensor_copy(
                            w_row1[0:1, ts(sl, 512)], ps[0:1, 7, 0:512]), deps=[hmmq])
                    st["b7"] = hQrow
                    hQst = sc.op("sync", lambda e: e.dma_start(
                        rfl[RFL_QSQ:RFL_QSQ + 16, :], w_row1[0:1, :]), deps=[hQrow], dma=True)
                    hQld = sc.op("sync", lambda e: e.dma_start(
                        w_sqr[0:32, 1, :],
                        rfl[RFL_QSQ:RFL_QSQ + 16, :].unsqueeze(1).broadcast_to([16, 2, C])),
                        deps=[hQst], dma=True)
                    hLnq = sc.op("scalar", lambda e: e.activation(
                        R(4), w_sqr[0:32, 1, :], AF.Ln, bias=sb_eps[0:32, 1:2]), deps=[hQld])
                hPa = sc.op("vector", lambda e: e.scalar_tensor_tensor(
                    R(5), R(3), -0.5, R(2), ALU.mult, ALU.add),
                    deps=[hLnk, hLnb, hG])
                hPa = sc.op("vector", lambda e: e.tensor_tensor(R(5), R(5), R(1), ALU.subtract))
                hFa = sc.op("vector", lambda e: e.scalar_tensor_tensor(
                    R(6), R(3), -0.5, R(1), ALU.mult, ALU.add))
                hsa = sc.op("vector", lambda e: e.tensor_tensor(R(10), R(6), R(2), ALU.add))
                hesa = sc.op("scalar", lambda e: e.activation(R(8), R(10), AF.Exp),
                             deps=[hsa])
                hFe = heqe = None
                if outc:
                    hFe = sc.op("vector", lambda e: e.scalar_tensor_tensor(
                        R(7), R(4), -0.5, R(1), ALU.mult, ALU.add), deps=[hLnq])
                    hFe = sc.op("vector", lambda e: e.tensor_scalar(
                        R(7), R(7), LNHALF_DK, None, ALU.add))
                    heqe = sc.op("scalar", lambda e: e.activation(R(9), R(7), AF.Exp),
                                 deps=[hFe])
                hsk2 = sc.op("vector", lambda e: e.tensor_scalar(
                    R(12), R(5), R(1)[:, 127:128], None, ALU.add), deps=[hPa])
                hesk = sc.op("scalar", lambda e: e.activation(R(13), R(12), AF.Exp),
                             deps=[hsk2])
                hlC = sc.op("scalar", lambda e: e.activation(
                    R(10)[:, 0:1], R(1)[:, 127:128], AF.Exp), deps=[hG, hsa])
                hB1 = sc.op("sync", lambda e: e.dma_start(
                    rfl[RFL_PA:RFL_PA + 96, :].rearrange("(s l) t -> l s t", s=3),
                    w_rows[0:32, 5:8, :]),
                    deps=[hPa, hFa, hFe, (hFe if outc else hFa),
                          st["rflsync"], st["rflgp"]], dma=True)
                hB2 = sc.op("sync", lambda e: e.dma_start(
                    rfl[RFL_ESA:RFL_ESA + 64, :].rearrange("(s l) t -> l s t", s=2),
                    w_rows[0:32, 8:10, :]),
                    deps=[hesa, (heqe if outc else hesa), st["rflgp"]], dma=True)
                hB3 = sc.op("sync", lambda e: e.dma_start(
                    rfl[RFL_LC:RFL_LC + 32, 0:1], R(10)[:, 0:1],
                    allow_slow_non_contiguous=True),
                    deps=[hlC, st["lcfree"]], dma=True)
                hlcl = sc.op("sync", lambda e: e.dma_start(
                    w_row1[0:1, 0:32], rfl[RFL_LC:RFL_LC + 32, 0:1],
                    allow_slow_non_contiguous=True), deps=[hB3], dma=True)
                st["lcfree"] = hlcl
                def kkmms(src2):
                    def f(e):
                        last = None
                        for h in range(16):
                            last = e.matmul(ps[:, h // 4, ts(h % 4)], w_k[:, h, :],
                                            src2[:, h, :], start=True, stop=True, skip_group_check=True)
                        return last
                    return f
                hKK = sc.op("tensor", kkmms(w_k), deps=[hLk, st["b03"]])
                pvKK3 = ps[:, 0:4, :].rearrange("p a b -> p (a b)").rearrange(
                    "p (a b) -> p a b", a=16)
                mview = lambda i: sb_masks[:, i, :].unsqueeze(1).broadcast_to([128, 16, 128])
                hKKs = sc.op("vector", lambda e: e.tensor_tensor(
                    w_KKs[:, :, :], pvKK3, mview(0), ALU.mult),
                    deps=[hKK, st["kkfree"]])
                hKKl = sc.op("vector", lambda e: e.tensor_tensor(
                    w_KKl[:, :, :], pvKK3, mview(2), ALU.mult))
                hKQe = None
                if outc:
                    hKQ = sc.op("tensor", kkmms(w_q), deps=[hLq, hKKl])
                    hKQe = sc.op("vector", lambda e: e.tensor_tensor(
                        w_KQ[:, :, :], pvKK3, mview(1), ALU.mult),
                        deps=[hKQ])
                st["b03"] = hKQe if outc else hKKl
                if ch == 0 and "sRFL2" in dbg_outs:
                    st["gfree"] = st["gfree"] + [sc.op("sync", lambda e: e.dma_start(
                        dbg_outs["sRFL2"], rfl[RFL_ESA:RFL_ESA + 64, :]),
                        deps=[hB2], dma=True)]
                if ch == 0 and "sRFL1" in dbg_outs:
                    st["gfree"] = st["gfree"] + [sc.op("sync", lambda e: e.dma_start(
                        dbg_outs["sRFL1"], rfl[RFL_PA:RFL_PA + 96, :]),
                        deps=[hB1], dma=True)]
                if ch == 0 and "sKK" in dbg_outs:
                    st["gfree"] = st["gfree"] + [sc.op("gpsimd", lambda e: e.dma_start(
                        dbg_outs["sKK"],
                        w_KKs.rearrange("p a b -> p (a b)")), deps=[hKKs], dma=True)]
                hTs = {}
                for slot, (ridx, hdep) in enumerate([(5, hPa), (6, hFa), (13, hesk)]):
                    htp = sc.op("tensor", lambda e, ridx=ridx: e.transpose(
                        ps[:, 6, 0:32], w_rows[0:32, ridx, :], sb_identf[0:32, 0:32]),
                        deps=[hdep, st["b6"]])
                    hTs[slot] = sc.op("scalar", lambda e, slot=slot: e.copy(
                        w_PT[:, slot, :], ps[:, 6, 0:32]), deps=[htp, st["ptfree"]])
                    st["b6"] = hTs[slot]
                hlmm = sc.op("tensor", lambda e: e.matmul(
                    ps[:, 6, 0:32], sb_onesrf[:, :], w_row1[0:1, 0:32],
                    start=True, stop=True), deps=[hlcl, st["b6"], st["supd"]])
                hlbc = sc.op("scalar", lambda e: e.copy(w_lbc[:, :], ps[:, 6, 0:32]),
                             deps=[hlmm])
                st["b6"] = hlbc

                newgfree = []
                hzs = hqp = None
                for g in range(2):
                    L0 = 16 * g
                    KH = slice(8 * g, 8 * g + 8)
                    gfree = st["gfree"]
                    lane_kk = lambda t, KH=KH: t[:, KH, :].unsqueeze(2) \
                        .broadcast_to([128, 8, 2, 128])
                    def build_tile(flat_row0, pt_slot, kkmasked, dst, hrowdeps, freedep,
                                   name=""):
                        hdup = sc.op("sync", lambda e, flat_row0=flat_row0, L0=L0: e.dma_start(
                            w_F2[:, :],
                            rfl[flat_row0 + L0:flat_row0 + L0 + 16, :]
                            .rearrange("a b -> (a b)").unsqueeze(0)
                            .broadcast_to([128, 16 * C])),
                            deps=hrowdeps + gfree + [st["f2free"]], dma=True)
                        st["rflsync"] = hdup
                        hts = None
                        for li in range(16):
                            hts = sc.op("gpsimd", lambda e, li=li, pt_slot=pt_slot, L0=L0: e.tensor_scalar(
                                w_ARG[:, ts(li)], w_F2[:, ts(li)],
                                w_PT[:, pt_slot, L0 + li:L0 + li + 1], CLAMP,
                                ALU.add, ALU.min),
                                deps=[hdup, hTs[pt_slot], st["argfree"]] + gfree)
                        st["f2free"] = hts
                        hexp = sc.op("scalar", lambda e, dst=dst: e.activation(
                            dst, w_ARG[:, :], AF.Exp), deps=[hts, freedep])
                        st["argfree"] = hexp
                        htap = None
                        tapk = "sEXP" + name
                        if ch == 0 and g == 0 and tapk in dbg_outs:
                            htap = sc.op("gpsimd", lambda e, tapk=tapk, dst=dst:
                                e.dma_start(dbg_outs[tapk], dst),
                                deps=[hexp], dma=True)
                            st["gfree"] = st["gfree"] + [htap]
                        d4 = dst.rearrange("p (a b c) -> p a b c", a=8, b=2)
                        hmul = sc.op("gpsimd", lambda e, d4=d4, kkmasked=kkmasked,
                                     lane_kk=lane_kk: e.tensor_tensor(
                            d4, d4, lane_kk(kkmasked), ALU.mult),
                            deps=[hexp, htap, (hKKs if kkmasked is w_KKs else
                                         (hKKl if kkmasked is w_KKl else hKQe))])
                        return hmul
                    hA = build_tile(RFL_FA, 0, w_KKs, w_AT[:, :], [hB1],
                                    st["atfree"], "A")
                    hBt = build_tile(RFL_PA, 1, w_KKl, w_B0[:, :], [hB1],
                                     st["b0free"], "B")
                    hE = None
                    if outc:
                        hE = build_tile(RFL_FE, 0, w_KQ, w_ET[:, :], [hB1],
                                        st["etfree"], "E")
                    hdsa = sc.op("gpsimd", lambda e, L0=L0: e.dma_start(
                        w_dupb[:, :],
                        rfl[RFL_ESA + L0:RFL_ESA + L0 + 16, :]
                        .rearrange("a b -> (a b)").unsqueeze(0)
                        .broadcast_to([128, 16 * C])),
                        deps=[hB2, st["dupbfree"]] + gfree, dma=True)
                    st["rflgp"] = hdsa
                    hdtap = None
                    if ch == 0 and g == 0 and "sDup" in dbg_outs:
                        hdtap = sc.op("gpsimd", lambda e: e.dma_start(
                            dbg_outs["sDup"], w_dupb[:, :]), deps=[hdsa], dma=True)
                    hkp = sc.op("gpsimd", lambda e, lane_kk=lane_kk: e.tensor_tensor(
                        w_kp.rearrange("p (a b c) -> p a b c", a=8, b=2),
                        w_dupb.rearrange("p (a b c) -> p a b c", a=8, b=2),
                        lane_kk(w_k), ALU.mult),
                        deps=[hdsa, hLk, st["kpfree"], hdtap])
                    hqp = None
                    if outc:
                        hdqe = sc.op("gpsimd", lambda e, L0=L0: e.dma_start(
                            w_dupb[:, :],
                            rfl[RFL_EQE + L0:RFL_EQE + L0 + 16, :]
                            .rearrange("a b -> (a b)").unsqueeze(0)
                            .broadcast_to([128, 16 * C])),
                            deps=[hB2, hkp], dma=True)
                        st["rflgp"] = hdqe
                        hqp = sc.op("gpsimd", lambda e, lane_kk=lane_kk: e.tensor_tensor(
                            w_qp.rearrange("p (a b c) -> p a b c", a=8, b=2),
                            w_dupb.rearrange("p (a b c) -> p a b c", a=8, b=2),
                            lane_kk(w_q), ALU.mult),
                            deps=[hdqe, hLq, st["qpfree"]])
                        st["dupbfree"] = hqp
                    else:
                        st["dupbfree"] = hkp
                    ktv = w_kt.rearrange("p (a c) -> p a c", a=16)
                    hkps = None
                    for li in range(16):
                        hkps = sc.op("gpsimd", lambda e, li=li, L0=L0, ktv=ktv: e.tensor_scalar(
                            w_kpp[:, ts(li)], ktv[:, (L0 + li) // 2, :],
                            w_PT[:, 2, L0 + li:L0 + li + 1], None, ALU.mult),
                            deps=[hLkt, hTs[2], st["supd"]])
                    hRm = sc.op("tensor", lambda e, L0=L0: (
                        [e.matmul(ps[:, li // 4, ts(li % 4)], w_kp[:, ts(li)],
                                  sb_Sb[:, L0 + li, :], start=True, stop=True, skip_group_check=True)
                         for li in range(16)][-1]),
                        deps=[hkp, st["b03"], st["supd"]])
                    st["kpfree"] = hRm
                    pv03 = ps[:, 0:4, :].rearrange("p a b -> p (a b)")
                    hRev = sc.op("vector", lambda e, g=g: e.tensor_tensor(
                        w_X[:, :], w_vt[:, ts(g, 2048)], pv03, ALU.subtract),
                        deps=[hRm, hLvt])
                    st["b03"] = hRev
                    hCev = {0: hA}
                    hBev = {0: hBt}
                    cb03 = st["b03"]
                    cb47 = st["b47"]
                    for k in range(1, 7):
                        Cp = w_AT[:, :] if k == 1 else w_pw[:, k - 2, :]
                        Bp = w_B0[:, :] if k == 1 else w_pb[:, (k - 1) % 2, :]
                        hCm = sc.op("tensor", lambda e, Cp=Cp, Bp=Bp: (
                            [e.matmul(ps[:, li // 4, ts(li % 4)], Bp[:, ts(li)],
                                      Cp[:, ts(li)], start=True, stop=True, skip_group_check=True)
                             for li in range(16)][-1]),
                            deps=[hCev[k - 1], hBev[k - 1], cb03])
                        hCev[k] = sc.op("vector", lambda e, k=k: e.tensor_copy(
                            w_pw[:, k - 1, :], pv03), deps=[hCm])
                        cb03 = hCev[k]
                        if k < 6:
                            hBm = sc.op("tensor", lambda e, Cp=Cp, Bp=Bp: (
                                [e.matmul(ps[:, 4 + li // 4, ts(li % 4)], Cp[:, ts(li)],
                                          Bp[:, ts(li)], start=True, stop=True, skip_group_check=True)
                                 for li in range(16)][-1]),
                                deps=[hCev[k - 1], hBev[k - 1], cb47])
                            hBev[k] = sc.op("scalar", lambda e, k=k: e.copy(
                                w_pb[:, k % 2, :],
                                ps[:, 4:8, :].rearrange("p a b -> p (a b)")), deps=[hBm])
                            cb47 = hBev[k]
                    st["b47"] = cb47
                    st["b0free"] = hBev[1]
                    happ = hRev
                    for ai, k in enumerate([6, 5, 4, 3, 2, 1, 0]):
                        Ck = w_AT[:, :] if k == 0 else w_pw[:, k - 1, :]
                        hXm = sc.op("tensor", lambda e, Ck=Ck: (
                            [e.matmul(ps[:, li // 4, ts(li % 4)], Ck[:, ts(li)],
                                      w_X[:, ts(li)], start=True, stop=True, skip_group_check=True)
                             for li in range(16)][-1]),
                            deps=[happ, cb03, (hCev[k] if k > 0 else hA)])
                        if k > 0:
                            happ = sc.op("vector", lambda e: e.tensor_tensor(
                                w_X[:, :], w_X[:, :], pv03, ALU.add), deps=[hXm])
                        else:
                            happ = sc.op("vector", lambda e: e.scalar_tensor_tensor(
                                w_X[:, :], pv03, -1.0, w_X[:, :], ALU.mult, ALU.add),
                                deps=[hXm])
                        cb03 = happ
                    st["b03"] = happ
                    st["atfree"] = happ
                    hy = None
                    if outc:
                        hOm = sc.op("tensor", lambda e, L0=L0: (
                            [(e.matmul(ps[:, li // 4, ts(li % 4)], sb_Sb[:, L0 + li, :],
                                       w_qp[:, ts(li)], start=True, stop=False, skip_group_check=True),
                              e.matmul(ps[:, li // 4, ts(li % 4)], w_X[:, ts(li)],
                                       w_ET[:, ts(li)], start=False, stop=True, skip_group_check=True))[-1]
                             for li in range(16)][-1]),
                            deps=[happ, hqp, hE, st["b03"], st["supd"]])
                        st["etfree"] = hOm
                        st["qpfree"] = hOm
                        hzs = sc.op("scalar", lambda e, L0=L0: e.activation(
                            w_zsil[:, :], w_z[:, L0:L0 + 16, :], AF.Silu),
                            deps=[hLz, st["zsilfree"]])
                        hxg = sc.op("vector", lambda e: e.tensor_tensor(
                            w_xg[:, :], pv03, w_zsil[:, :], ALU.mult), deps=[hOm, hzs])
                        st["b03"] = hxg
                        st["zsilfree"] = hxg
                        hgq = sc.op("scalar", lambda e: e.activation(
                            w_gsq.rearrange("p (a b) -> p a b", a=16), 
                            w_xg.rearrange("p (a b) -> p a b", a=16), AF.Square),
                            deps=[hxg])
                        hrr = None
                        for sl in range(4):
                            hrm = sc.op("tensor", lambda e, sl=sl: e.matmul(
                                ps[0:1, 7, 0:512], sb_ones1[:, :], w_gsq[:, ts(sl, 512)],
                                start=True, stop=True),
                                deps=[hgq, st["b7"], hrr, hlmm, hQld])
                            hrr = sc.op("vector", lambda e, sl=sl: e.tensor_copy(
                                w_row1[0:1, ts(sl, 512)], ps[0:1, 7, 0:512]), deps=[hrm])
                        st["b7"] = hrr
                        hrl = sc.op("scalar", lambda e: e.activation(
                            w_row1[0:1, :], w_row1[0:1, :], AF.Ln, scale=1.0 / 128,
                            bias=sb_eps[0:1, 1:2]), deps=[hrr])
                        hre = sc.op("scalar", lambda e: e.activation(
                            w_row1[0:1, :], w_row1[0:1, :], AF.Exp, scale=-0.5))
                        hrc = sc.op("vector", lambda e: e.tensor_copy(
                            w_rowb[0:1, :], w_row1[0:1, :]), deps=[hre])
                        st["row1cons"] = hrc
                        hbcm = sc.op("tensor", lambda e: (
                            [e.matmul(ps[:, 4 + sl, 0:512], sb_onesr[:, :],
                                      w_rowb[0:1, ts(sl, 512)], start=True, stop=True, skip_group_check=True)
                             for sl in range(4)][-1]),
                            deps=[hrc, st["b47"]])
                        pv47 = ps[:, 4:8, :].rearrange("p a b -> p (a b)")
                        hy = sc.op("vector", lambda e: e.scalar_tensor_tensor(
                            w_y[:, :], w_xg[:, :], sb_anwc[:, :], pv47,
                            ALU.mult, ALU.mult), deps=[hbcm, st["ycons"]])
                        st["b47"] = hy
                        hyst = sc.op("sync", lambda e, L0=L0, tsl=tsl: e.dma_start(
                            ys_v[:, L0:L0 + 16, tsl], w_y.rearrange("p (a b) -> p a b", a=16)),
                            deps=[hy], dma=True)
                        newgfree.append(hyst)
                        st["ycons"] = hyst
                    hSm = sc.op("tensor", lambda e: (
                        [e.matmul(ps[:, li // 4, ts(li % 4)], w_kpp[:, ts(li)],
                                  w_X[:, ts(li)], start=True, stop=True, skip_group_check=True)
                         for li in range(16)][-1]),
                        deps=[hkps, happ, st["b03"], ((hy if outc else None))])
                    hSe = None
                    for li in range(16):
                        hSe = sc.op("vector", lambda e, li=li, L0=L0: e.scalar_tensor_tensor(
                            sb_S[:, L0 + li, :], sb_S[:, L0 + li, :],
                            w_lbc[:, L0 + li:L0 + li + 1],
                            ps[:, li // 4, ts(li % 4)], ALU.mult, ALU.add),
                            deps=[hSm, hlbc])
                    hSb = sc.op("vector", lambda e, L0=L0: e.tensor_copy(
                        sb_Sb[:, L0:L0 + 16, :], sb_S[:, L0:L0 + 16, :]), deps=[hSe])
                    st["b03"] = hSb
                    st["supd"] = hSb
                    newgfree.extend([hSb, happ])
                    if ch == 0 and g == 0:
                        tap0 = len(newgfree)
                        if "sX" in dbg_outs:
                            newgfree.append(sc.op("gpsimd", lambda e: e.dma_start(
                                dbg_outs["sX"], w_X[:, :]), deps=[happ], dma=True))
                        if "sA" in dbg_outs:
                            newgfree.append(sc.op("gpsimd", lambda e: e.dma_start(
                                dbg_outs["sA"], w_AT[:, :]), deps=[hA], dma=True))
                        if "sKp" in dbg_outs:
                            newgfree.append(sc.op("gpsimd", lambda e: e.dma_start(
                                dbg_outs["sKp"], w_kp[:, :]), deps=[hkp], dma=True))
                        if "sKpp" in dbg_outs:
                            newgfree.append(sc.op("gpsimd", lambda e: e.dma_start(
                                dbg_outs["sKpp"], w_kpp[:, :]), deps=[hkps], dma=True))
                        if "sBAS" in dbg_outs:
                            newgfree.append(sc.op("sync", lambda e: e.dma_start(
                                dbg_outs["sBAS"], bas[:, 0:128]),
                                deps=[hLb, hLa], dma=True))
                        if "sBA" in dbg_outs:
                            newgfree.append(sc.op("sync", lambda e: e.dma_start(
                                dbg_outs["sBA"][0:32, :],
                                w_ba[0:32, :, :].rearrange("p a b -> p (a b)")),
                                deps=[hLb, hLa], dma=True))
                        if "sRows" in dbg_outs:
                            newgfree.append(sc.op("sync", lambda e: e.dma_start(
                                dbg_outs["sRows"][0:32, :],
                                w_rows[0:32, :, :].rearrange("p a b -> p (a b)")),
                                deps=[hesk, hB1, hB2], dma=True))
                        if outc:
                            if "sET" in dbg_outs:
                                newgfree.append(sc.op("gpsimd", lambda e: e.dma_start(
                                    dbg_outs["sET"], w_ET[:, :]), deps=[hy], dma=True))
                            if "sZsil" in dbg_outs:
                                newgfree.append(sc.op("gpsimd", lambda e: e.dma_start(
                                    dbg_outs["sZsil"], w_zsil[:, :]), deps=[hy], dma=True))
                            if "sQp" in dbg_outs:
                                newgfree.append(sc.op("gpsimd", lambda e: e.dma_start(
                                    dbg_outs["sQp"], w_qp[:, :]), deps=[hy], dma=True))
                            if "sXg" in dbg_outs:
                                newgfree.append(sc.op("gpsimd", lambda e: e.dma_start(
                                    dbg_outs["sXg"], w_xg[:, :]), deps=[hy], dma=True))
                            if "sY" in dbg_outs:
                                newgfree.append(sc.op("gpsimd", lambda e: e.dma_start(
                                    dbg_outs["sY"], w_y[:, :]), deps=[hy], dma=True))
                            if "sRow1" in dbg_outs:
                                newgfree.append(sc.op("gpsimd", lambda e: e.dma_start(
                                    dbg_outs["sRow1"], w_row1[0:1, :]), deps=[hy], dma=True))
                        tapsh = newgfree[tap0:]
                        if tapsh:
                            hbar = sc.op("gpsimd", lambda e: e.memset(
                                sb_iota[0:1, 0:1], 0.0),
                                deps=tapsh + [st["supd"], st["atfree"],
                                              st["b0free"], st["kpfree"],
                                              st["b7"], st["etfree"],
                                              st["qpfree"], st["zsilfree"]])
                            st["supd"] = hbar
                            st["atfree"] = hbar
                            st["b0free"] = hbar
                            st["kpfree"] = hbar
                            st["argfree"] = hbar
                            st["dupbfree"] = hbar
                            st["b7"] = hbar
                            st["etfree"] = hbar
                            st["qpfree"] = hbar
                            st["zsilfree"] = hbar
                    tapch = max(int(os.environ.get("TAPSTOP", "0")) - 1, 0)
                    if ch == tapch and g == 1 and "sS" in dbg_outs:
                        newgfree.append(sc.op("sync", lambda e: e.dma_start(
                            dbg_outs["sS"], sb_S[:, :, :].rearrange("p a b -> p (a b)")),
                            deps=[hSb], dma=True))
                st["gfree"] = newgfree
                st["kkfree"] = hE if outc else hBt
                st["ptfree"] = hkps
                st["kcons"] = [hkp, (hKQe if outc else hKKl)]
                st["ktcons"] = [hkps]
                st["vtcons"] = [hRev]
                st["bacons"] = [hLnb, hSg]
                st["qcons"] = ([hqp] if outc else st["qcons"])
                st["zcons"] = ([hzs] if outc else st["zcons"])
                st["rowscons"] = [hTs[0], hTs[1], hTs[2], hB1, hB2, hB3, hesk]

            def scan_phase(front_done):
                fd = list(front_done)
                st = {"kcons": fd, "ktcons": fd, "vtcons": fd, "bacons": fd,
                      "qcons": fd, "zcons": fd, "rowscons": fd, "gfree": fd,
                      "supd": sc.barrier_handle("vector"),
                      "b03": None, "b47": None, "b6": None, "b7": None,
                      "argfree": None, "dupbfree": None, "ycons": None,
                      "row1cons": None, "f2free": None, "rflsync": None,
                      "rflgp": None, "lcfree": None, "kkfree": None,
                      "ptfree": None, "atfree": None, "b0free": None,
                      "etfree": None, "kpfree": None, "qpfree": None,
                      "zsilfree": None}
                sso = [] if role == 0 else list(range(NC0))
                osc = list(range(NC0)) if role == 0 else list(range(NC0, NC0 + NC1))
                tstop = int(os.environ.get("TAPSTOP", "0"))
                if tstop > 0:
                    for chn in range(tstop):
                        scan_chunk(sc, chn, True, st)
                    return st
                for chn in sso:
                    scan_chunk(sc, chn, False, st)
                for chn in osc:
                    scan_chunk(sc, chn, True, st)
                return st

            def epilogue_phase(all_done):
                t0r = 0 if role == 0 else P
                ngrp = NC0 if role == 0 else NC1
                pv03 = ps[:, 0:4, :].rearrange("p a b -> p (a b)")
                mcol = lambda i: e1_st[:, i:i + 1]

                hwo = None
                for kt in range(32):
                    hwo = sc.op("gpsimd", lambda e, kt=kt: e.dma_start(
                        e1_wo[:, kt, :], wo_v2[kt]), deps=all_done, dma=True)
                hrw = sc.op("gpsimd", lambda e: e.dma_start(e1_rw[:, :, :], rw_v),
                            deps=all_done, dma=True)
                hrb = sc.op("gpsimd", lambda e: e.dma_start(
                    e1_rb[0:1, :], router_b), deps=all_done, dma=True)

                ngrp1 = int(os.environ.get("E1N", "0")) or ngrp
                st1 = {"ysfree": all_done, "xbfree": all_done, "b03": None,
                       "b45": None, "b6": None, "l2Tcons": [], "x1free": [],
                       "l2bfree": []}
                for m in range(ngrp1):
                    gt0 = t0r + m * 128
                    lt0 = m * 128
                    hys = sc.op("sync", lambda e, gt0=gt0: e.dma_start(
                        e1_ys[:, :, :], ys_v[:, :, gt0:gt0 + 128]),
                        deps=st1["ysfree"], dma=True)
                    hxb = sc.op("sync", lambda e, gt0=gt0: e.dma_start(
                        e1_xb[:, :], xbt[gt0:gt0 + 128, :]),
                        deps=list(st1["xbfree"]), dma=True)
                    hmm = sc.op("tensor", lambda e: (
                        [[e.matmul(ps[:, sl, :], e1_ys[:, h, :],
                                   e1_wo[:, h, ts(sl, 512)],
                                   start=(h == 0), stop=(h == 31),
                                   skip_group_check=True)
                          for h in range(32)][-1] for sl in range(4)][-1]),
                        deps=[hys, hwo, st1["b03"]])
                    hevc = sc.op("scalar", lambda e: e.copy(e1_x1[:, :], pv03),
                                 deps=[hmm] + st1["x1free"])
                    st1["b03"] = hevc
                    hsum = sc.op("vector", lambda e: e.tensor_reduce(
                        mcol(0), e1_x1[:, :], AX.X, ALU.add), deps=[hevc])
                    hsq = sc.op("scalar", lambda e: e.activation(
                        e1_sq2[:, :], e1_x1[:, :], AF.Square, accum_out=mcol(1)),
                        deps=[hevc, hmm])
                    hmean = sc.op("vector", lambda e: e.tensor_scalar(
                        mcol(2), mcol(0), 1.0 / D, None, ALU.mult), deps=[hsum])
                    hmsq = sc.op("vector", lambda e: e.tensor_scalar(
                        mcol(3), mcol(2), mcol(2), None, ALU.mult))
                    ht1 = sc.op("vector", lambda e: e.tensor_scalar(
                        mcol(4), mcol(1), 1.0 / D, None, ALU.mult), deps=[hsq])
                    hvar = sc.op("vector", lambda e: e.tensor_tensor(
                        mcol(4), mcol(4), mcol(3), ALU.subtract))
                    hrs = sc.op("scalar", lambda e: e.activation(
                        mcol(5), mcol(4), AF.Ln, bias=sb_eps[:, 0:1]),
                        deps=[hvar])
                    hrs = sc.op("scalar", lambda e: e.activation(
                        mcol(5), mcol(5), AF.Exp, scale=-0.5))
                    hx1 = sc.op("vector", lambda e: e.tensor_scalar(
                        e1_x1[:, :], e1_x1[:, :], mcol(2), mcol(5),
                        ALU.subtract, ALU.mult), deps=[hrs, hsq])
                    hx1 = sc.op("vector", lambda e: e.tensor_tensor(
                        e1_x1[:, :], e1_x1[:, :], e1_xb[:, :], ALU.add),
                        deps=[hxb])
                    st1["xbfree"] = [hx1]
                    hx1st = sc.op("sync", lambda e, lt0=lt0: e.dma_start(
                        x1s[lt0:lt0 + 128, :], e1_x1[:, :]), deps=[hx1], dma=True)
                    hsum2 = sc.op("vector", lambda e: e.tensor_reduce(
                        mcol(0), e1_x1[:, :], AX.X, ALU.add), deps=[hx1])
                    hsq2 = sc.op("scalar", lambda e: e.activation(
                        e1_sq2[:, :], e1_x1[:, :], AF.Square, accum_out=mcol(1)),
                        deps=[hx1, hmm])
                    hmean2 = sc.op("vector", lambda e: e.tensor_scalar(
                        mcol(2), mcol(0), 1.0 / D, None, ALU.mult), deps=[hsum2])
                    hmsq2 = sc.op("vector", lambda e: e.tensor_scalar(
                        mcol(3), mcol(2), mcol(2), None, ALU.mult))
                    ht12 = sc.op("vector", lambda e: e.tensor_scalar(
                        mcol(4), mcol(1), 1.0 / D, None, ALU.mult), deps=[hsq2])
                    hvar2 = sc.op("vector", lambda e: e.tensor_tensor(
                        mcol(4), mcol(4), mcol(3), ALU.subtract))
                    hrs2 = sc.op("scalar", lambda e: e.activation(
                        mcol(5), mcol(4), AF.Ln, bias=sb_eps[:, 0:1]),
                        deps=[hvar2])
                    hrs2 = sc.op("scalar", lambda e: e.activation(
                        mcol(5), mcol(5), AF.Exp, scale=-0.5))
                    hl2 = sc.op("vector", lambda e: e.tensor_scalar(
                        e1_l2b[:, :], e1_x1[:, :], mcol(2), mcol(5),
                        ALU.subtract, ALU.mult), deps=[hrs2] + st1["l2bfree"])
                    st1["x1free"] = [hl2, hx1st]
                    hl2st = sc.op("sync", lambda e, lt0=lt0: e.dma_start(
                        ln2s[lt0:lt0 + 128, :], e1_l2b[:, :]), deps=[hl2],
                        dma=True)
                    htr = sc.op("tensor", lambda e: (
                        [e.transpose(psb[:, 4 + kt // 8, (kt % 8) * 128:
                                         (kt % 8) * 128 + 128],
                                     e1_l2b[:, ts(kt)], sb_ident[:, :])
                         for kt in range(16)][-1]),
                        deps=[hl2, st1["b45"]])
                    hev = sc.op("scalar", lambda e: e.copy(
                        e1_l2T.rearrange("p a b -> p (a b)"),
                        psb[:, 4:6, :].rearrange("p a b -> p (a b)")),
                        deps=[htr] + st1["l2Tcons"])
                    st1["b45"] = hev
                    st1["l2bfree"] = [htr, hl2st]
                    hl2Tst = sc.op("sync", lambda e, lt0=lt0: e.dma_start(
                        l2ts_v[:, :, lt0:lt0 + 128], e1_l2T[:, :, :]),
                        deps=[hev], dma=True)
                    hrm = sc.op("tensor", lambda e: (
                        [e.matmul(ps[:, 6, 0:8], e1_l2T[:, kt, :],
                                  e1_rw[:, kt, :], start=(kt == 0), stop=False,
                                  skip_group_check=True)
                         for kt in range(16)][-1],
                        e.matmul(ps[:, 6, 0:8], sb_onesr[:, :], e1_rb[0:1, :],
                                 start=False, stop=True, skip_group_check=True))[-1],
                        deps=[hev, hrw, hrb, st1["b6"]])
                    hlg = sc.op("vector", lambda e: e.tensor_copy(
                        e1_lg[:, :], ps[:, 6, 0:8]), deps=[hrm])
                    st1["b6"] = hlg
                    hm1 = sc.op("vector", lambda e: e.tensor_reduce(
                        mcol(6), e1_lg[:, :], AX.X, ALU.max))
                    hk1 = sc.op("vector", lambda e: e.tensor_scalar(
                        e1_t8a[:, :], e1_lg[:, :], mcol(6), None, ALU.is_equal))
                    hlp = sc.op("vector", lambda e: e.scalar_tensor_tensor(
                        e1_t8b[:, :], e1_t8a[:, :], -1e9, e1_lg[:, :],
                        ALU.mult, ALU.add))
                    hm2 = sc.op("vector", lambda e: e.tensor_reduce(
                        mcol(7), e1_t8b[:, :], AX.X, ALU.max))
                    hsel = sc.op("vector", lambda e: e.tensor_scalar(
                        e1_t8a[:, :], e1_lg[:, :], mcol(7), None, ALU.is_ge))
                    hsh = sc.op("vector", lambda e: e.tensor_scalar(
                        e1_t8b[:, :], e1_lg[:, :], mcol(6), None, ALU.subtract))
                    hex = sc.op("scalar", lambda e: e.activation(
                        e1_t8b[:, :], e1_t8b[:, :], AF.Exp),
                        deps=[sc.barrier_handle("vector")])
                    hwe = sc.op("vector", lambda e: e.tensor_tensor(
                        e1_t8b[:, :], e1_t8b[:, :], e1_t8a[:, :], ALU.mult),
                        deps=[hex])
                    hden = sc.op("vector", lambda e: e.tensor_reduce(
                        mcol(6), e1_t8b[:, :], AX.X, ALU.add))
                    hrc = sc.op("scalar", lambda e: e.activation(
                        mcol(7), mcol(6), AF.Ln), deps=[hden])
                    hrc = sc.op("scalar", lambda e: e.activation(
                        mcol(7), mcol(7), AF.Exp, scale=-1.0))
                    hfw = sc.op("vector", lambda e: e.tensor_scalar(
                        e1_t8b[:, :], e1_t8b[:, :], mcol(7), None, ALU.mult),
                        deps=[hrc])
                    hgst = sc.op("sync", lambda e, lt0=lt0: e.dma_start(
                        gates[lt0:lt0 + 128, :], e1_t8b[:, :]), deps=[hfw],
                        dma=True)
                    st1["ysfree"] = [hmm, hsq2]
                    st1["l2Tcons"] = [hrm, hl2Tst]

                e1_done = [sc.barrier_handle(e) for e in ENGS] + \
                          [sc.barrier_handle("sync", dma=True),
                           sc.barrier_handle("gpsimd", dma=True)]
                if os.environ.get("EPSTOP", "0") == "1":
                    return

                m2c = lambda i: e2_st[:, i:i + 1]
                hfnr = sc.op("sync", lambda e: e.dma_start(
                    e2_fr[0:1, :], fnorm_w), deps=e1_done, dma=True)
                hfnm = sc.op("tensor", lambda e: (
                    [e.matmul(ps[:, sl, :], sb_onesrf[:, :],
                              e2_fr[0:1, ts(sl, 512)], start=True, stop=True,
                              skip_group_check=True) for sl in range(4)][-1]),
                    deps=[hfnr] + e1_done)
                hfnb = sc.op("vector", lambda e: e.tensor_copy(
                    e2_fnb[:, :], pv03), deps=[hfnm])

                sgs = []
                i0 = 0
                while i0 < ngrp:
                    sgs.append(list(range(i0, min(i0 + GSG, ngrp))))
                    i0 += GSG
                st2 = {"l2Tfree": e1_done, "combfree": [hfnb], "b03": hfnb,
                       "b45": None, "hfree": None,
                       "wfree": [list(e1_done), list(e1_done)],
                       "ebfree": list(e1_done), "gtfree": e1_done,
                       "xfree": e1_done[0], "x1free": e1_done[0],
                       "frfree": [hfnb], "cTfree": None, "yfree": None}
                for sg in sgs:
                    G = len(sg)
                    hl2L, hgtL, hcz = {}, {}, {}
                    for gi, m in enumerate(sg):
                        lt0 = m * 128
                        hl2L[gi] = sc.op("sync", lambda e, gi=gi, lt0=lt0: e.dma_start(
                            e2_l2T[:, gi, :, :], l2ts_v[:, :, lt0:lt0 + 128]),
                            deps=st2["l2Tfree"], dma=True)
                        hgtL[gi] = sc.op("sync", lambda e, gi=gi, lt0=lt0: e.dma_start(
                            e2_gt[:, gi, :], gates[lt0:lt0 + 128, :]),
                            deps=st2["gtfree"], dma=True)
                        hcz[gi] = sc.op("vector", lambda e, gi=gi: e.memset(
                            e2_comb[:, gi, :], 0.0), deps=st2["combfree"])
                    hacc = {gi: hcz[gi] for gi in range(G)}
                    for ei in range(NE):
                        heb = sc.op("gpsimd", lambda e, ei=ei: e.dma_start(
                            e2_eb[0:1, :], expert_b[ei:ei + 1, :]),
                            deps=st2["ebfree"], dma=True)
                        hw = {}
                        for hf in range(2):
                            hw[hf] = sc.op("gpsimd", lambda e, ei=ei, hf=hf: e.dma_start(
                                e2_w[:, hf, :, :], we_v2[ei, hf]),
                                deps=st2["wfree"][hf], dma=True)
                        for gi in range(G):
                            hm = sc.op("tensor", lambda e, gi=gi: (
                                [[e.matmul(ps[:, sl, :], e2_l2T[:, gi, kt, :],
                                           e2_w[:, kt // 8, kt % 8, ts(sl, 512)],
                                           start=(kt == 0), stop=False,
                                           skip_group_check=True)
                                  for kt in range(16)][-1] for sl in range(4)][-1],
                                [e.matmul(ps[:, sl, :], sb_onesr[:, :],
                                          e2_eb[0:1, ts(sl, 512)], start=False,
                                          stop=True, skip_group_check=True)
                                 for sl in range(4)][-1])[-1],
                                deps=[hw[0], hw[1], heb, hl2L[gi], st2["b03"]])
                            hsl = sc.op("scalar", lambda e: e.activation(
                                e2_h[:, :], pv03, AF.Silu),
                                deps=[hm, st2["hfree"]])
                            hac = sc.op("vector", lambda e, gi=gi, ei=ei:
                                e.scalar_tensor_tensor(
                                    e2_comb[:, gi, :], e2_h[:, :],
                                    e2_gt[:, gi, ei:ei + 1],
                                    e2_comb[:, gi, :], ALU.mult, ALU.add),
                                deps=[hsl, hacc[gi], hgtL[gi]])
                            hacc[gi] = hac
                            st2["b03"] = hsl
                            st2["hfree"] = hac
                        tb = sc.barrier_handle("tensor")
                        st2["wfree"] = [[tb], [tb]]
                        st2["ebfree"] = [tb]
                    hpw = {}
                    for hf in range(2):
                        hpw[hf] = sc.op("gpsimd", lambda e, hf=hf: e.dma_start(
                            e2_w[:, hf, :, :], pw_v2[hf]),
                            deps=st2["wfree"][hf], dma=True)
                    hpb = sc.op("gpsimd", lambda e: e.dma_start(
                        e2_eb[0:1, :], proj_b), deps=st2["ebfree"], dma=True)
                    for gi, m in enumerate(sg):
                        lt0 = m * 128
                        htr = sc.op("tensor", lambda e, gi=gi: (
                            [e.transpose(psb[:, 4 + kt // 8, (kt % 8) * 128:
                                             (kt % 8) * 128 + 128],
                                         e2_comb[:, gi, ts(kt)], sb_ident[:, :])
                             for kt in range(16)][-1]),
                            deps=[hacc[gi], st2["b45"]])
                        hev = sc.op("scalar", lambda e: e.copy(
                            e2_cT.rearrange("p a b -> p (a b)"),
                            psb[:, 4:6, :].rearrange("p a b -> p (a b)")),
                            deps=[htr, st2["cTfree"]])
                        st2["b45"] = hev
                        hpm = sc.op("tensor", lambda e: (
                            [[e.matmul(ps[:, sl, :], e2_cT[:, kt, :],
                                       e2_w[:, kt // 8, kt % 8, ts(sl, 512)],
                                       start=(kt == 0), stop=False,
                                       skip_group_check=True)
                              for kt in range(16)][-1] for sl in range(4)][-1],
                            [e.matmul(ps[:, sl, :], sb_onesr[:, :],
                                      e2_eb[0:1, ts(sl, 512)], start=False,
                                      stop=True, skip_group_check=True)
                             for sl in range(4)][-1])[-1],
                            deps=[hev, hpw[0], hpw[1], hpb, st2["b03"]])
                        st2["cTfree"] = hpm
                        hx2 = sc.op("sync", lambda e, lt0=lt0: e.dma_start(
                            e2_x[:, :], ln2s[lt0:lt0 + 128, :]),
                            deps=[st2["xfree"]], dma=True)
                        hx1L = sc.op("sync", lambda e, lt0=lt0: e.dma_start(
                            e2_x1[:, :], x1s[lt0:lt0 + 128, :]),
                            deps=[st2["x1free"]], dma=True)
                        hy = sc.op("vector", lambda e: e.tensor_tensor(
                            e2_y[:, :], pv03, e2_x[:, :], ALU.add),
                            deps=[hpm, hx2, st2["yfree"], st2["hfree"]])
                        st2["b03"] = hy
                        hq = sc.op("scalar", lambda e: e.activation(
                            e2_fr[:, :], e2_y[:, :], AF.Square,
                            accum_out=m2c(0)), deps=[hy] + st2["frfree"])
                        hms = sc.op("vector", lambda e: e.tensor_scalar(
                            m2c(1), m2c(0), 1.0 / D, None, ALU.mult), deps=[hq])
                        hrst = sc.op("scalar", lambda e: e.activation(
                            m2c(1), m2c(1), AF.Ln, bias=sb_eps[:, 1:2]),
                            deps=[hms])
                        hrst = sc.op("scalar", lambda e: e.activation(
                            m2c(1), m2c(1), AF.Exp, scale=-0.5))
                        hfr = sc.op("vector", lambda e: e.tensor_scalar(
                            e2_fr[:, :], e2_y[:, :], m2c(1), None, ALU.mult),
                            deps=[hrst])
                        hfr = sc.op("vector", lambda e: e.tensor_tensor(
                            e2_fr[:, :], e2_fr[:, :], e2_fnb[:, :], ALU.mult))
                        hs1 = sc.op("vector", lambda e: e.tensor_reduce(
                            m2c(2), e2_fr[:, :], AX.X, ALU.add))
                        hq2 = sc.op("scalar", lambda e: e.activation(
                            e2_y[:, :], e2_fr[:, :], AF.Square,
                            accum_out=m2c(3)), deps=[hfr])
                        hmn = sc.op("vector", lambda e: e.tensor_scalar(
                            m2c(4), m2c(2), 1.0 / D, None, ALU.mult), deps=[hs1])
                        hms2 = sc.op("vector", lambda e: e.tensor_scalar(
                            m2c(5), m2c(4), m2c(4), None, ALU.mult))
                        hv2 = sc.op("vector", lambda e: e.tensor_scalar(
                            m2c(6), m2c(3), 1.0 / D, None, ALU.mult), deps=[hq2])
                        hv2 = sc.op("vector", lambda e: e.tensor_tensor(
                            m2c(6), m2c(6), m2c(5), ALU.subtract))
                        hr2 = sc.op("scalar", lambda e: e.activation(
                            m2c(7), m2c(6), AF.Ln, bias=sb_eps[:, 0:1]),
                            deps=[hv2])
                        hr2 = sc.op("scalar", lambda e: e.activation(
                            m2c(7), m2c(7), AF.Exp, scale=-0.5))
                        ho = sc.op("vector", lambda e: e.tensor_scalar(
                            e2_y[:, :], e2_fr[:, :], m2c(4), m2c(7),
                            ALU.subtract, ALU.mult), deps=[hr2, hq2])
                        ho = sc.op("vector", lambda e: e.tensor_tensor(
                            e2_y[:, :], e2_y[:, :], e2_x1[:, :], ALU.add),
                            deps=[hx1L])
                        st2["frfree"] = [ho]
                        host = sc.op("sync", lambda e, lt0=lt0: e.dma_start(
                            out[lt0:lt0 + 128, :], e2_y[:, :]), deps=[ho],
                            dma=True)
                        st2["yfree"] = host
                        st2["xfree"] = hy
                        st2["x1free"] = ho
                    st2["l2Tfree"] = [sc.barrier_handle("tensor")]
                    st2["combfree"] = [sc.barrier_handle("tensor")]
                    st2["gtfree"] = [sc.barrier_handle("vector")]
                    st2["wfree"] = [[hpm], [hpm]]
                    st2["ebfree"] = [hpm]

            lnxT_free = None
            prev_banks = None
            fs = {}
            for ip, (tst, ntk, full) in enumerate(passes):
                if stage == "A" and ip > 0:
                    break
                hlast_mm, hsc_barrier, fs = front_pass(tst, ntk, full, lnxT_free,
                                                       prev_banks, fs)
                lnxT_free = hlast_mm
                prev_banks = hsc_barrier
                if ip == 0 and "lnxT" in dbg_outs:
                    aev = sc.barrier_handle("scalar")
                    o0 = tst - max(tst - 128, 0)
                    for kt in range(16):
                        sc.op("gpsimd", lambda e, kt=kt, o0=o0: e.dma_start(
                            dbg_outs["lnxT"][ts(kt), 0:1280], v_lnxT[:, kt, o0:o0 + 1280]),
                            deps=[aev], dma=True)

            if stage not in ("A", "front"):
                front_done = [sc.barrier_handle(e) for e in ENGS] + \
                             [sc.barrier_handle("sync", dma=True),
                              sc.barrier_handle("gpsimd", dma=True)]
                stS = scan_phase(front_done)
                if stage == "all" and int(os.environ.get("TAPSTOP", "0")) == 0:
                    scan_done = [sc.barrier_handle(e) for e in ENGS] + \
                                [sc.barrier_handle("sync", dma=True),
                                 sc.barrier_handle("gpsimd", dma=True)]
                    epilogue_phase(scan_done)

            front_all = [sc.barrier_handle(e) for e in ENGS] + \
                        [sc.barrier_handle("sync", dma=True),
                         sc.barrier_handle("gpsimd", dma=True)]

            def tap(name, emitters):
                if name not in dbg_outs:
                    return
                for fn in emitters:
                    sc.op("gpsimd", fn, deps=front_all, dma=True)

            tap("qs", [lambda e, hh=hh: e.dma_start(
                dbg_outs["qs"][ts(hh), :], qs_v[:, hh, 0:1280]) for hh in range(16)])
            tap("kfs", [lambda e, hh=hh: e.dma_start(
                dbg_outs["kfs"][ts(hh), :], kfs_v[:, hh, 0:1280]) for hh in range(16)])
            tap("zs", [lambda e, hh=hh: e.dma_start(
                dbg_outs["zs"][ts(hh), :], zs_v[:, hh, 0:1280]) for hh in range(32)])
            tap("kts", [lambda e, n=n: e.dma_start(
                dbg_outs["kts"][ts(n), :], kts_t[n]) for n in range(10)])
            tap("vts", [lambda e, n=n: e.dma_start(
                dbg_outs["vts"][ts(n), :], vts_t[n]) for n in range(10)])
            tap("bas", [lambda e: e.dma_start(dbg_outs["bas"][:, :], bas[:, 0:1280])])
            yo = 0 if role == 0 else P
            tap("ys", [lambda e, hh=hh, yo=yo: e.dma_start(
                dbg_outs["ys"][ts(hh), :], ys_v[:, hh, yo:yo + 1280]) for hh in range(32)])
            tap("x1r", [lambda e: e.dma_start(dbg_outs["x1r"], x1s[0:256, :])])
            tap("l2r", [lambda e: e.dma_start(dbg_outs["l2r"], ln2s[0:256, :])])
            tap("gt", [lambda e: e.dma_start(dbg_outs["gt"], gates[0:256, :])])
            tap("l2tr", [lambda e: e.dma_start(dbg_outs["l2tr"], l2ts[0:256, 0:256])])

            fin = [sc.barrier_handle(e) for e in ENGS] + \
                  [sc.barrier_handle(e, dma=True) for e in ("sync", "gpsimd", "scalar")]
            if (stage != "all" or int(os.environ.get("TAPSTOP", "0")) > 0
                    or os.environ.get("EPSTOP", "0") == "1"):
                hout = sc.op("sync", lambda e: e.dma_start(out[0:128, :], v_x[:, 0, :]),
                             deps=fin, dma=True)
                sc.op("sync", lambda e: e.nop(), deps=[hout])
            else:
                sc.op("sync", lambda e: e.nop(), deps=fin)
            return sc

        NOBRANCH = os.environ.get("NOBRANCH", "0") == "1"
        scripts = {0: build_script(0)}
        if not NOBRANCH:
            scripts[1] = build_script(1)

        def section(engname):
            def body(eng):
                if NOBRANCH:
                    scripts[0].replay(engname, eng)
                    return
                r = eng.to_reg(eng.partition_id() % 2)
                with eng.If_eq(r, 0):
                    scripts[0].replay(engname, eng)
                with eng.Else():
                    scripts[1].replay(engname, eng)
            return body

        block.sync(section("sync"))
        block.gpsimd(section("gpsimd"))
        block.tensor(section("tensor"))
        block.scalar(section("scalar"))
        block.vector(section("vector"))

    return nc

import numpy as np
from concourse.bass_utils import run_bass_kernel_spmd

_NC_CACHE = {}


def kernel(**inputs):
    x = np.asarray(inputs["L0_attn_hidden_states"], np.float32)
    conv_w = np.asarray(inputs["L0_attn_conv_w"], np.float32).reshape(8192, 4)
    expert_w = np.asarray(inputs["L0_ffn_expert_w"], np.float32).reshape(8 * 2048, 2048)
    base = {
        "w_qkvz": np.asarray(inputs["L0_attn_w_qkvz"], np.float32),
        "w_ba": np.asarray(inputs["L0_attn_w_ba"], np.float32),
        "conv_w": conv_w,
        "dt_bias": np.asarray(inputs["L0_attn_dt_bias"], np.float32).reshape(32, 1),
        "a_log": np.asarray(inputs["L0_attn_A_log"], np.float32).reshape(32, 1),
        "anorm_w": np.asarray(inputs["L0_attn_norm_w"], np.float32).reshape(128, 1),
        "w_out": np.asarray(inputs["L0_attn_w_out"], np.float32),
        "router_w": np.asarray(inputs["L0_ffn_router_w"], np.float32),
        "router_b": np.asarray(inputs["L0_ffn_router_b"], np.float32).reshape(1, 8),
        "expert_w": expert_w,
        "expert_b": np.asarray(inputs["L0_ffn_expert_b"], np.float32),
        "proj_w": np.asarray(inputs["L0_ffn_proj_w"], np.float32),
        "proj_b": np.asarray(inputs["L0_ffn_proj_b"], np.float32).reshape(1, 2048),
        "fnorm_w": np.asarray(inputs["L0_ffn_norm_w"], np.float32).reshape(1, 2048),
    }
    if "nc" not in _NC_CACHE:
        _NC_CACHE["nc"] = build(dbg=[])
    nc = _NC_CACHE["nc"]
    in_maps = []
    for r in range(8):
        m = dict(base)
        m["xb"] = np.ascontiguousarray(x[r // 2])
        in_maps.append(m)
    res = run_bass_kernel_spmd(nc, in_maps, core_ids=list(range(8)))
    full = np.empty((4, 4096, 2048), np.float32)
    for b in range(4):
        full[b, 0:T0] = res.results[2 * b]["out"]
        full[b, P:S] = res.results[2 * b + 1]["out"][0:T1]
    return full


# version 02 (final)
# speedup vs baseline: 1.0281x; 1.0281x over previous
import sys
sys.path.insert(0, '/opt/trn_rl_repo')
import contextlib
import os
import numpy as np
import concourse.bass as bass
import concourse.mybir as mybir

F32 = mybir.dt.float32
BF16 = mybir.dt.bfloat16
AF = mybir.ActivationFunctionType
ALU = mybir.AluOpType
AX = mybir.AxisListType

S, D = 4096, 2048
P = 2432
C = 128
NKH, NVH, DK, DV = 16, 32, 128, 128
KDIM, VDIM = NKH * DK, NVH * DV
NE = 8
T0, T1 = P, S - P
NC0, NC1 = T0 // C, T1 // C
LN_EPS, RMS_EPS, L2_EPS, FFN_EPS = 1e-5, 1e-6, 1e-6, 1e-6
CLAMP = 30.0
LNHALF_DK = -0.5 * float(np.log(128.0))
MAXPASS = 1280

ENGS = ("sync", "gpsimd", "tensor", "scalar", "vector")
BLANKET = ("gpsimd", "scalar", "vector")


def ts(i, n=128):
    return slice(i * n, (i + 1) * n)


class Script:
    def __init__(self, sems):
        self.items = {e: [] for e in ENGS}
        self.cnt = {}
        self.sems = sems

    def _bump(self, sem, amt):
        self.cnt[sem] = self.cnt.get(sem, 0) + amt
        return (sem, self.cnt[sem])

    def op(self, eng, emit, deps=(), dma=False):
        waits = {}
        for d in deps:
            if d is None:
                continue
            smn, v = d
            waits[smn] = max(waits.get(smn, 0), v)
        csem = eng + ("_d" if dma else "_c")
        if not dma and (eng in BLANKET or eng == "tensor"):
            waits.pop(csem, None)
        if not dma and eng in BLANKET:
            waits[csem] = self.cnt.get(csem, 0)
        wl = sorted(waits.items())
        h = self._bump(csem, 16 if dma else 1)
        self.items[eng].append((emit, wl, [(csem, 16 if dma else 1)]))
        return h

    def barrier_handle(self, eng, dma=False):
        csem = eng + ("_d" if dma else "_c")
        return (csem, self.cnt.get(csem, 0))

    def replay(self, engname, eng):
        for emit, waits, incs in self.items[engname]:
            for smn, v in waits:
                if v > 0:
                    eng.wait_ge(self.sems[smn], v)
            inst = emit(eng)
            for smn, amt in incs:
                inst.then_inc(self.sems[smn], amt)


def build(dbg=(), stage="all"):
    nc = bass.Bass("TRN2", target_bir_lowering=False, debug=False)

    xbt = nc.dram_tensor("xb", [S, D], F32, kind="ExternalInput").ap()
    w_qkvz = nc.dram_tensor("w_qkvz", [D, 12288], F32, kind="ExternalInput").ap()
    w_ba = nc.dram_tensor("w_ba", [D, 64], F32, kind="ExternalInput").ap()
    conv_w = nc.dram_tensor("conv_w", [2 * KDIM + VDIM, 4], F32, kind="ExternalInput").ap()
    dt_bias = nc.dram_tensor("dt_bias", [NVH, 1], F32, kind="ExternalInput").ap()
    a_log = nc.dram_tensor("a_log", [NVH, 1], F32, kind="ExternalInput").ap()
    anorm_w = nc.dram_tensor("anorm_w", [DV, 1], F32, kind="ExternalInput").ap()
    w_out = nc.dram_tensor("w_out", [VDIM, D], F32, kind="ExternalInput").ap()
    router_w = nc.dram_tensor("router_w", [D, NE], F32, kind="ExternalInput").ap()
    router_b = nc.dram_tensor("router_b", [1, NE], F32, kind="ExternalInput").ap()
    expert_w = nc.dram_tensor("expert_w", [NE * D, D], F32, kind="ExternalInput").ap()
    expert_b = nc.dram_tensor("expert_b", [NE, D], F32, kind="ExternalInput").ap()
    proj_w = nc.dram_tensor("proj_w", [D, D], F32, kind="ExternalInput").ap()
    proj_b = nc.dram_tensor("proj_b", [1, D], F32, kind="ExternalInput").ap()
    fnorm_w = nc.dram_tensor("fnorm_w", [1, D], F32, kind="ExternalInput").ap()
    out = nc.dram_tensor("out", [T0, D], F32, kind="ExternalOutput").ap()

    wq_cc = w_qkvz.rearrange("(kt p) (cc c) -> cc p kt c", p=128, c=128)
    wba_v = w_ba.rearrange("(kt p) c -> p kt c", p=128)
    cw_cc = conv_w.rearrange("(cc p) j -> cc p j", p=128)
    we_v = expert_w.rearrange("(ekt p) c -> ekt p c", p=128)
    wo_v = w_out.rearrange("(kt p) c -> kt p c", p=128)
    pw_v = proj_w.rearrange("(kt p) c -> kt p c", p=128)
    rw_v = router_w.rearrange("(kt p) c -> p kt c", p=128)

    qs = nc.dram_tensor("qs", [KDIM, S], BF16, kind="Internal").ap()
    kfs = nc.dram_tensor("kfs", [KDIM, S], BF16, kind="Internal").ap()
    kts_d = nc.dram_tensor("kts", [S, KDIM], BF16, kind="Internal").ap()
    vts_d = nc.dram_tensor("vts", [S, VDIM], BF16, kind="Internal").ap()
    zs = nc.dram_tensor("zs", [VDIM, S], BF16, kind="Internal").ap()
    bas = nc.dram_tensor("bas", [64, S], F32, kind="Internal").ap()
    ys = nc.dram_tensor("ys", [VDIM, S], BF16, kind="Internal").ap()
    x1s = nc.dram_tensor("x1s", [T0, D], F32, kind="Internal").ap()
    ln2s = nc.dram_tensor("ln2s", [T0, D], BF16, kind="Internal").ap()
    l2ts = nc.dram_tensor("l2ts", [D, T0], BF16, kind="Internal").ap()
    ewb = nc.dram_tensor("ewb", [NE * D, D], BF16, kind="Internal").ap()
    pwb = nc.dram_tensor("pwb", [D, D], BF16, kind="Internal").ap()
    gates = nc.dram_tensor("gates", [T0, NE], F32, kind="Internal").ap()
    rfl = nc.dram_tensor("rfl", [224, C], F32, kind="Internal").ap()

    qs_v = qs.rearrange("(h p) s -> p h s", p=128)
    kfs_v = kfs.rearrange("(h p) s -> p h s", p=128)
    zs_v = zs.rearrange("(h p) s -> p h s", p=128)
    ys_v = ys.rearrange("(h p) s -> p h s", p=128)
    kts_t = kts_d.rearrange("(n p) c -> n p c", p=128)
    vts_t = vts_d.rearrange("(n p) c -> n p c", p=128)
    bas_hf = bas.rearrange("(h f) s -> h f s", f=4)
    l2ts_v = l2ts.rearrange("(kt p) s -> p kt s", p=128)
    we_v2 = ewb.rearrange("(e h j p) c -> e h p j c", h=2, j=8, p=128)
    pw_v2 = pwb.rearrange("(h j p) c -> h p j c", h=2, j=8, p=128)
    wo_v2 = w_out.rearrange("(kt p) c -> kt p c", p=128)

    dbg_outs = {}
    for name, shape in dbg:
        dbg_outs[name] = nc.dram_tensor("dbg_" + name, list(shape), F32,
                                        kind="ExternalOutput").ap()

    with contextlib.ExitStack() as ctx:
        E = ctx.enter_context

        sb_ident = E(nc.sbuf_tensor("sb_ident", [128, 128], BF16))
        sb_ones1 = E(nc.sbuf_tensor("sb_ones1", [128, 1], BF16))
        sb_onesr = E(nc.sbuf_tensor("sb_onesr", [1, 128], BF16))
        sb_onesrf = E(nc.sbuf_tensor("sb_onesrf", [1, 128], F32))
        sb_masks = E(nc.sbuf_tensor("sb_masks", [128, 3, 128], BF16))
        sb_identf = E(nc.sbuf_tensor("sb_identf", [128, 128], F32))
        sb_iota = E(nc.sbuf_tensor("sb_iota", [128, 128], F32))
        sb_alog = E(nc.sbuf_tensor("sb_alog", [32, 1], F32))
        sb_dtb = E(nc.sbuf_tensor("sb_dtb", [32, 1], F32))
        sb_negea = E(nc.sbuf_tensor("sb_negea", [32, 1], F32))
        sb_anwc = E(nc.sbuf_tensor("sb_anwc", [128, 1], F32))
        sb_z32 = E(nc.sbuf_tensor("sb_z32", [32, 128], F32))
        sb_eps = E(nc.sbuf_tensor("sb_eps", [128, 4], F32))
        sb_S = E(nc.sbuf_tensor("sb_S", [128, NVH, DV], F32))
        sb_Sb = E(nc.sbuf_tensor("sb_Sb", [128, NVH, DV], BF16))

        AR = 168 * 1024
        sb_ar = E(nc.sbuf_tensor("sb_ar", [128, AR], mybir.dt.uint8))

        def av(off, dtype, shape):
            esz = mybir.dt.size(dtype)
            n = int(np.prod(shape))
            assert off % 4 == 0 and off + n * esz <= AR, (off, n * esz, AR)
            v = sb_ar[:, off:off + n * esz].bitcast(dtype)
            if len(shape) > 1:
                names = " ".join(f"d{i}" for i in range(len(shape)))
                v = v.rearrange(f"p ({names}) -> p {names}",
                                **{f"d{i}": shape[i] for i in range(len(shape))})
            return v

        ps = E(nc.psum_tensor("ps", [128, 8, 512], F32))
        psb = ps.bitcast(BF16)

        semnames = [e + "_c" for e in ENGS] + [e + "_d" for e in ("sync", "gpsimd", "scalar")]
        sems = {n: E(nc.semaphore(n)) for n in semnames}
        block = E(nc.Block())

        _o = [0]

        def alloc(n):
            r = (_o[0] + 3) & ~3
            _o[0] = r + n
            assert _o[0] <= AR, _o[0]
            return r

        A_x = alloc(2 * D * 4)
        A_ln = alloc(2 * D * 2)
        A_st = alloc(2 * 8 * 4)
        A_dump = alloc(D * 4)
        A_lnxT = alloc(16 * (MAXPASS + 128) * 2)
        F_w = alloc(2 * 16 * 128 * 2)
        F_cw = alloc(2 * 4 * 4)
        F_proj = alloc(2 * (MAXPASS + 3) * 2)
        F_conv = alloc(2 * MAXPASS * 4)
        F_post = alloc(2 * MAXPASS * 2)
        F_tt = alloc(10 * 128 * 2)
        F_ba = alloc(MAXPASS * 4)
        FRONT_END = _o[0]

        v_x = av(A_x, F32, [2, D])
        v_ln = av(A_ln, BF16, [2, D])
        v_st = av(A_st, F32, [2, 8])
        v_dump = av(A_dump, F32, [D])
        v_lnxT = av(A_lnxT, BF16, [16, MAXPASS + 128])
        v_w = av(F_w, BF16, [2, 16, 128])
        v_cw = av(F_cw, F32, [2, 4])
        v_proj = av(F_proj, BF16, [2, MAXPASS + 3])
        v_conv = av(F_conv, F32, [2, MAXPASS])
        v_post = av(F_post, BF16, [2, MAXPASS])
        v_tt = av(F_tt, BF16, [10, 128])
        v_barow = av(F_ba, F32, [MAXPASS])

        _o[0] = 0
        SCq = alloc(NKH * C * 2)
        SCk = alloc(NKH * C * 2)
        SCkt = alloc(KDIM * 2)
        SCvt = alloc(VDIM * 2)
        SCz = alloc(NVH * C * 2)
        SCba = alloc(2 * C * 4)
        SCrows = alloc(14 * C * 4)
        SCsq2 = alloc(2048 * 2)
        SCsqr = alloc(2 * C * 4)
        SCAT = alloc(16 * C * 2)
        SCB0 = alloc(16 * C * 2)
        SCET = alloc(16 * C * 2)
        SCKKs = alloc(NKH * C * 2)
        SCKKl = alloc(NKH * C * 2)
        SCKQ = alloc(NKH * C * 2)
        SCF2 = alloc(16 * C * 4)
        SCARG = alloc(16 * C * 4)
        SCdupb = alloc(16 * C * 2)
        SCPT = alloc(3 * 32 * 4)
        SCpw = alloc(6 * 16 * C * 2)
        SCpb = alloc(2 * 16 * C * 2)
        SCX = alloc(16 * C * 2)
        SCkp = alloc(16 * C * 2)
        SCqp = alloc(16 * C * 2)
        SCkpp = alloc(16 * DK * 2)
        SCzsil = alloc(16 * C * 2)
        SCxg = alloc(16 * C * 2)
        SCgsq = alloc(16 * C * 2)
        SCy = alloc(16 * C * 2)
        SCrow1 = alloc(2048 * 4)
        SCrowb = alloc(2048 * 2)
        SClbc = alloc(32 * 4)
        SCAN_END = _o[0]

        w_q = av(SCq, BF16, [NKH, C])
        w_k = av(SCk, BF16, [NKH, C])
        w_kt = av(SCkt, BF16, [KDIM])
        w_vt = av(SCvt, BF16, [VDIM])
        w_z = av(SCz, BF16, [NVH, C])
        w_ba = av(SCba, F32, [2, C])
        w_rows = av(SCrows, F32, [14, C])
        w_sq2 = av(SCsq2, BF16, [2048])
        w_sqr = av(SCsqr, F32, [2, C])
        w_AT = av(SCAT, BF16, [16 * C])
        w_B0 = av(SCB0, BF16, [16 * C])
        w_ET = av(SCET, BF16, [16 * C])
        w_KKs = av(SCKKs, BF16, [NKH, C])
        w_KKl = av(SCKKl, BF16, [NKH, C])
        w_KQ = av(SCKQ, BF16, [NKH, C])
        w_F2 = av(SCF2, F32, [16 * C])
        w_ARG = av(SCARG, F32, [16 * C])
        w_dupb = av(SCdupb, BF16, [16 * C])
        w_PT = av(SCPT, F32, [3, 32])
        w_pw = av(SCpw, BF16, [6, 16 * C])
        w_pb = av(SCpb, BF16, [2, 16 * C])
        w_X = av(SCX, BF16, [16 * C])
        w_kp = av(SCkp, BF16, [16 * C])
        w_qp = av(SCqp, BF16, [16 * C])
        w_kpp = av(SCkpp, BF16, [16 * DK])
        w_zsil = av(SCzsil, BF16, [16 * C])
        w_xg = av(SCxg, BF16, [16 * C])
        w_gsq = av(SCgsq, BF16, [16 * C])
        w_y = av(SCy, BF16, [16 * C])
        w_row1 = av(SCrow1, F32, [2048])
        w_rowb = av(SCrowb, BF16, [2048])
        w_lbc = av(SClbc, F32, [32])

        _o[0] = 0
        E1wo = alloc(32 * 2048 * 2)
        E1ys = alloc(32 * 128 * 2)
        E1xb = alloc(2048 * 4)
        E1x1 = alloc(2048 * 4)
        E1l2b = alloc(2048 * 2)
        E1l2T = alloc(16 * 128 * 2)
        E1rw = alloc(16 * 8 * 2)
        E1rb = alloc(8 * 2)
        E1lg = alloc(8 * 4)
        E1t8a = alloc(8 * 4)
        E1t8b = alloc(8 * 4)
        E1st = alloc(8 * 4)
        assert _o[0] <= AR

        e1_wo = av(E1wo, BF16, [32, 2048])
        e1_ys = av(E1ys, BF16, [32, 128])
        e1_sq2 = av(E1ys, F32, [2048])
        e1_xb = av(E1xb, F32, [2048])
        e1_x1 = av(E1x1, F32, [2048])
        e1_l2b = av(E1l2b, BF16, [2048])
        e1_l2T = av(E1l2T, BF16, [16, 128])
        e1_rw = av(E1rw, BF16, [16, 8])
        e1_rb = av(E1rb, BF16, [8])
        e1_lg = av(E1lg, F32, [8])
        e1_t8a = av(E1t8a, F32, [8])
        e1_t8b = av(E1t8b, F32, [8])
        e1_st = av(E1st, F32, [8])

        GSG = 7
        _o[0] = 0
        E2w = alloc(2 * 8 * 2048 * 2)
        E2l2T = alloc(GSG * 16 * 128 * 2)
        E2comb = alloc(GSG * 2048 * 2)
        E2h = alloc(2048 * 4)
        E2cT = alloc(16 * 128 * 2)
        E2x = alloc(2048 * 2)
        E2x1 = alloc(2048 * 4)
        E2fr = alloc(2048 * 4)
        E2fnb = alloc(2048 * 4)
        E2eb = alloc(2048 * 2)
        E2gt = alloc(GSG * 8 * 4)
        E2st = alloc(8 * 4)
        assert _o[0] <= AR

        e2_w = av(E2w, BF16, [2, 8, 2048])
        e2_l2T = av(E2l2T, BF16, [GSG, 16, 128])
        e2_comb = av(E2comb, BF16, [GSG, 2048])
        e2_h = av(E2h, BF16, [2048])
        e2_y = av(E2h, F32, [2048])
        e2_cT = av(E2cT, BF16, [16, 128])
        e2_x = av(E2x, BF16, [2048])
        e2_x1 = av(E2x1, F32, [2048])
        e2_fr = av(E2fr, F32, [2048])
        e2_fnb = av(E2fnb, F32, [2048])
        e2_eb = av(E2eb, BF16, [2048])
        e2_gt = av(E2gt, F32, [GSG, 8])
        e2_st = av(E2st, F32, [8])

        RFL_KSQ, RFL_QSQ, RFL_PA, RFL_FA, RFL_FE, RFL_ESA, RFL_EQE, RFL_LC = \
            0, 16, 32, 64, 96, 128, 160, 192

        def build_script(role):
            sc = Script(sems)
            passes = ([(0, 1280, True), (1280, 1152, True)] if role == 0 else
                      [(0, 1280, False), (1280, 1152, False),
                       (2432, 1280, True), (3712, 384, True)])

            last_iota_reader = [None]

            def iota_mask(dst, opk):
                hi = sc.op("gpsimd", lambda e: e.iota(
                    sb_iota[:, :], [[1, 128]], base=0, channel_multiplier=-1,
                    allow_small_or_imprecise_dtypes=True), deps=[last_iota_reader[0]])
                sc.op("vector", lambda e, opk=opk: e.tensor_single_scalar(
                    sb_iota[:, :], sb_iota[:, :], 0.0, opk), deps=[hi])
                hd = sc.op("vector", lambda e, dst=dst: e.tensor_copy(dst, sb_iota[:, :]))
                last_iota_reader[0] = hd
                return hd

            iota_mask(sb_ident[:, :], ALU.is_equal)
            iota_mask(sb_masks[:, 0, :], ALU.is_gt)
            iota_mask(sb_masks[:, 1, :], ALU.is_ge)
            iota_mask(sb_masks[:, 2, :], ALU.is_lt)
            sc.op("vector", lambda e: e.memset(sb_ones1[:, :], 1.0))
            sc.op("vector", lambda e: e.memset(sb_onesr[:, :], 1.0))
            sc.op("vector", lambda e: e.memset(sb_onesrf[:, :], 1.0))
            sc.op("vector", lambda e: e.memset(sb_z32[:, :], 0.0))
            sc.op("vector", lambda e: e.memset(sb_eps[:, 0:1], LN_EPS))
            sc.op("vector", lambda e: e.memset(sb_eps[:, 1:2], 1e-6))
            sc.op("vector", lambda e: e.memset(sb_eps[:, 2:3], 1.0))
            sc.op("vector", lambda e: e.memset(sb_S[:, :, :], 0.0))
            sc.op("vector", lambda e: e.memset(sb_Sb[:, :, :], 0.0))
            hi2 = sc.op("gpsimd", lambda e: e.iota(
                sb_iota[:, :], [[1, 128]], base=0, channel_multiplier=-1,
                allow_small_or_imprecise_dtypes=True), deps=[last_iota_reader[0]])
            sc.op("vector", lambda e: e.tensor_single_scalar(
                sb_identf[:, :], sb_iota[:, :], 0.0, ALU.is_equal), deps=[hi2])
            h1 = sc.op("sync", lambda e: e.dma_start(sb_alog[:, :], a_log), dma=True)
            sc.op("sync", lambda e: e.dma_start(sb_dtb[:, :], dt_bias), dma=True)
            sc.op("sync", lambda e: e.dma_start(sb_anwc[:, :], anorm_w), dma=True)
            h4 = sc.op("scalar", lambda e: e.activation(sb_negea[:, :], sb_alog[:, :], AF.Exp),
                       deps=[h1])
            sc.op("vector", lambda e: e.tensor_scalar_mul(sb_negea[:, :], sb_negea[:, :], -1.0),
                  deps=[h4])
            consts_done = {e: sc.barrier_handle(e) for e in ENGS}

            def phaseA(tstart, ntok, lnxT_free_h, prev_pass_banks_h, pfs):
                astart = max(tstart - 128, 0)
                atiles = (tstart + ntok - astart) // 128
                hload, hnorm, htr, hev = {}, {}, {}, {}
                last_ev = None
                for m in range(atiles):
                    b = m % 2
                    row = astart // 128 + m
                    xdeps = ([hnorm.get(m - 2)] if m >= 2 else pfs.get("norm", []))
                    hload[m] = sc.op("sync", lambda e, b=b, row=row: e.dma_start(
                        v_x[:, b, :], xbt[ts(row), :]),
                        deps=xdeps + [consts_done["vector"]], dma=True)
                    hsq = sc.op("scalar", lambda e, b=b: e.activation(
                        v_dump[:, :], v_x[:, b, :], AF.Square, accum_out=v_st[:, b, 1:2]),
                        deps=[hload[m]])
                    sc.op("vector", lambda e, b=b: e.tensor_reduce(
                        v_st[:, b, 0:1], v_x[:, b, :], AX.X, ALU.add), deps=[hload[m]])
                    sc.op("vector", lambda e, b=b: e.tensor_scalar_mul(
                        v_st[:, b, 2:3], v_st[:, b, 0:1], 1.0 / D))
                    sc.op("vector", lambda e, b=b: e.tensor_scalar(
                        v_st[:, b, 4:5], v_st[:, b, 2:3], v_st[:, b, 2:3], None, ALU.mult))
                    sc.op("vector", lambda e, b=b: e.tensor_scalar(
                        v_st[:, b, 5:6], v_st[:, b, 1:2], 1.0 / D, None, ALU.mult), deps=[hsq])
                    hvar = sc.op("vector", lambda e, b=b: e.tensor_tensor(
                        v_st[:, b, 5:6], v_st[:, b, 5:6], v_st[:, b, 4:5], ALU.subtract))
                    sc.op("scalar", lambda e, b=b: e.activation(
                        v_st[:, b, 6:7], v_st[:, b, 5:6], AF.Ln, bias=sb_eps[:, 0:1]),
                        deps=[hvar])
                    hrs = sc.op("scalar", lambda e, b=b: e.activation(
                        v_st[:, b, 7:8], v_st[:, b, 6:7], AF.Exp, scale=-0.5))
                    ndeps = ([htr.get((m - 2, 15))] if m >= 2 else pfs.get("lasttr", []))
                    hnorm[m] = sc.op("vector", lambda e, b=b: e.tensor_scalar(
                        v_ln[:, b, :], v_x[:, b, :], v_st[:, b, 2:3], v_st[:, b, 7:8],
                        ALU.subtract, ALU.mult),
                        deps=[hrs] + ndeps)
                    for kt in range(16):
                        g = m * 16 + kt
                        tb = g % 2
                        htr[(m, kt)] = sc.op("tensor", lambda e, b=b, kt=kt, tb=tb: e.transpose(
                            psb[:, 6 + tb, 0:128], v_ln[:, b, ts(kt)], sb_ident[:, :]),
                            deps=[hnorm[m], hev.get(g - 2), consts_done["vector"],
                                  (prev_pass_banks_h if g < 2 else None)])
                        hev[g] = sc.op("scalar", lambda e, kt=kt, tb=tb, m=m: e.copy(
                            v_lnxT[:, kt, ts(m)], psb[:, 6 + tb, 0:128]),
                            deps=[htr[(m, kt)], lnxT_free_h])
                        last_ev = hev[g]
                phA_out = dict(norm=[hnorm[atiles - 1], hnorm.get(atiles - 2)],
                               lasttr=[htr[(atiles - 1, 15)], htr.get((atiles - 2, 15))])
                return astart, atiles, last_ev, phA_out

            def front_pass(tstart, ntok, full, lnxT_free_h, prev_pass_banks_h, pfs):
                astart, atiles, lnxT_ready, phA_out = phaseA(tstart, ntok, lnxT_free_h,
                                                              prev_pass_banks_h, pfs)
                halo = 3 if tstart > 0 else 0
                off = (tstart - astart) - halo
                ncols = ntok + halo
                nsl = (ncols + 511) // 512
                ntile = ntok // 128

                fam = []
                if full:
                    for h in range(16):
                        fam.append((6 * h, h, qs_v[:, h, tstart:tstart + ntok], None))
                for h in range(16):
                    fam.append((6 * h + 1, 16 + h, kfs_v[:, h, tstart:tstart + ntok], ("k", h)))
                for i in range(32):
                    fam.append((6 * (i // 2) + 2 + (i % 2), 32 + i, None, ("v", i)))
                if full:
                    for i in range(32):
                        fam.append((6 * (i // 2) + 4 + (i % 2), None,
                                    zs_v[:, i, tstart:tstart + ntok], None))

                hW, hmm, hpe, hsl, hcons = {}, {}, {}, {}, {}
                htr_last_store = pfs.get("trstore")
                last_tr_evac = None
                for cidx, (colc, convrow, fmdest, trspec) in enumerate(fam):
                    b = cidx % 2
                    has_conv = convrow is not None
                    wdeps = ([hmm.get(cidx - 2)] if cidx >= 2 else pfs.get("mms", []))
                    sldeps = ([hsl.get(cidx - 2)] if cidx >= 2 else pfs.get("sl", []))
                    hW[cidx] = sc.op("gpsimd", lambda e, b=b, colc=colc: e.dma_start(
                        v_w[:, b, :, :], wq_cc[colc]),
                        deps=wdeps, dma=True)
                    hcw = None
                    if has_conv:
                        hcw = sc.op("gpsimd", lambda e, b=b, cr=convrow: e.dma_start(
                            v_cw[:, b, :], cw_cc[cr]),
                            deps=sldeps, dma=True)

                    def mms(e, b=b, nsl=nsl, ncols=ncols, off=off):
                        last = None
                        for kt in range(16):
                            for sl in range(nsl):
                                c0 = sl * 512
                                cn = min(512, ncols - c0)
                                last = e.matmul(ps[:, b * 3 + sl, 0:cn], v_w[:, b, kt, :],
                                                v_lnxT[:, kt, off + c0: off + c0 + cn],
                                                start=(kt == 0), stop=(kt == 15))
                        return last
                    hmm[cidx] = sc.op("tensor", mms,
                                      deps=[hW[cidx], lnxT_ready, hpe.get(cidx - 2)])

                    def evac(e, b=b, nsl=nsl, ncols=ncols):
                        last = None
                        for sl in range(nsl):
                            c0 = sl * 512
                            cn = min(512, ncols - c0)
                            last = e.copy(v_proj[:, b, c0:c0 + cn], ps[:, b * 3 + sl, 0:cn])
                        return last
                    hpe[cidx] = sc.op("scalar", evac, deps=[hmm[cidx]] + sldeps)

                    if has_conv:
                        hcv = None
                        for j in range(4):
                            sh = halo + j - 3
                            def cop(e, b=b, j=j, sh=sh, ntok=ntok):
                                if sh >= 0:
                                    if j == 0:
                                        return e.tensor_scalar(
                                            v_conv[:, b, 0:ntok], v_proj[:, b, sh:sh + ntok],
                                            v_cw[:, b, 0:1], None, ALU.mult)
                                    return e.scalar_tensor_tensor(
                                        v_conv[:, b, 0:ntok], v_proj[:, b, sh:sh + ntok],
                                        v_cw[:, b, j:j + 1], v_conv[:, b, 0:ntok],
                                        ALU.mult, ALU.add)
                                t0j = -sh
                                if j == 0:
                                    e.memset(v_conv[:, b, 0:t0j], 0.0)
                                    return e.tensor_scalar(
                                        v_conv[:, b, t0j:ntok], v_proj[:, b, 0:ntok - t0j],
                                        v_cw[:, b, 0:1], None, ALU.mult)
                                return e.scalar_tensor_tensor(
                                    v_conv[:, b, t0j:ntok], v_proj[:, b, 0:ntok - t0j],
                                    v_cw[:, b, j:j + 1], v_conv[:, b, t0j:ntok],
                                    ALU.mult, ALU.add)
                            hcv = sc.op("vector", cop,
                                        deps=[hpe[cidx], hcw] + sldeps)
                        pcons = (hcons.get(cidx - 2, []) if cidx >= 2
                                 else pfs.get("cons", []))
                        hsl[cidx] = sc.op("scalar", lambda e, b=b, ntok=ntok: e.activation(
                            v_post[:, b, 0:ntok], v_conv[:, b, 0:ntok], AF.Silu),
                            deps=[hcv] + pcons)
                    else:
                        pcons = (hcons.get(cidx - 2, []) if cidx >= 2
                                 else pfs.get("cons", []))
                        hsl[cidx] = sc.op("scalar", lambda e, b=b, ntok=ntok, halo=halo: e.copy(
                            v_post[:, b, 0:ntok], v_proj[:, b, halo:halo + ntok]),
                            deps=[hpe[cidx]] + pcons)

                    consumers = []
                    if fmdest is not None:
                        consumers.append(sc.op("sync", lambda e, b=b, d=fmdest, ntok=ntok:
                                               e.dma_start(d, v_post[:, b, 0:ntok]),
                                               deps=[hsl[cidx]], dma=True))
                    if trspec is not None:
                        kind, hh = trspec
                        hte = {}
                        for tt in range(ntile):
                            g = tt % 2
                            htp = sc.op("tensor", lambda e, b=b, tt=tt, g=g: e.transpose(
                                psb[:, 6 + g, 0:128], v_post[:, b, ts(tt)], sb_ident[:, :]),
                                deps=[hsl[cidx], hte.get(tt - 2), htr_last_store,
                                      (lnxT_ready if tt < 2 else None)])
                            hte[tt] = sc.op("scalar", lambda e, tt=tt, g=g: e.copy(
                                v_tt[:, tt, :], psb[:, 6 + g, 0:128]), deps=[htp])
                        consumers.append(hte[ntile - 1])
                        dst = (kts_d if kind == "k" else vts_d)

                        def trstore(e, hh=hh, dst=dst, tstart=tstart, ntile=ntile):
                            dview = dst[tstart:tstart + ntile * 128, ts(hh)].rearrange(
                                "(n p) c -> p n c", p=128)
                            return e.dma_start(dview, v_tt[:, 0:ntile, :])
                        htr_last_store = sc.op("sync", trstore,
                                               deps=[hte[ntile - 1]], dma=True)
                        consumers.append(htr_last_store)
                    hcons[cidx] = consumers

                nf = len(fam)
                hWba = sc.op("gpsimd", lambda e: e.dma_start(v_w[:, 0, :, 0:64], wba_v),
                             deps=[hmm.get(nf - 2)], dma=True)

                def bamms(e, ntok=ntok, off=off, halo=halo):
                    last = None
                    nsl2 = (ntok + 511) // 512
                    for kt in range(16):
                        for sl in range(nsl2):
                            c0 = sl * 512
                            cn = min(512, ntok - c0)
                            last = e.matmul(ps[:, sl, 0:cn][0:64, :], v_w[:, 0, kt, 0:64],
                                            v_lnxT[:, kt, off + halo + c0: off + halo + c0 + cn],
                                            start=(kt == 0), stop=(kt == 15))
                    return last
                hbamm = sc.op("tensor", bamms,
                              deps=[hWba, lnxT_ready, hpe.get(nf - 1), hpe.get(nf - 2)])

                def baevac(e, ntok=ntok):
                    last = None
                    nsl2 = (ntok + 511) // 512
                    for sl in range(nsl2):
                        c0 = sl * 512
                        cn = min(512, ntok - c0)
                        last = e.copy(v_barow[0:64, c0:c0 + cn], ps[:, sl, 0:cn][0:64, :])
                    return last
                hbae = sc.op("scalar", baevac, deps=[hbamm, pfs.get("bast")])
                hbast = sc.op("sync", lambda e, tstart=tstart, ntok=ntok: e.dma_start(
                    bas[:, tstart:tstart + ntok], v_barow[0:64, 0:ntok]),
                    deps=[hbae], dma=True)
                nf2 = len(fam)
                fs = dict(mms=[hmm[nf2 - 1], hmm.get(nf2 - 2), hbamm],
                          sl=[hsl[nf2 - 1], hsl.get(nf2 - 2)],
                          cons=hcons.get(nf2 - 1, []) + hcons.get(nf2 - 2, []) +
                               [hbae],
                          bast=hbast, trstore=htr_last_store,
                          norm=phA_out["norm"], lasttr=phA_out["lasttr"])
                return hbamm, sc.barrier_handle("scalar"), fs

            def scan_chunk(sc, ch, outc, st):
                t0c = ch * C
                tsl = slice(t0c, t0c + C)
                hLk = sc.op("sync", lambda e, tsl=tsl: e.dma_start(
                    w_k[:, :, :], kfs_v[:, :, tsl]), deps=st["kcons"], dma=True)
                hLkt = sc.op("sync", lambda e, tsl=tsl: e.dma_start(
                    w_kt[:, :], kts_d[tsl, :]), deps=st["ktcons"], dma=True)
                hLvt = sc.op("sync", lambda e, tsl=tsl: e.dma_start(
                    w_vt[:, :], vts_d[tsl, :]), deps=st["vtcons"], dma=True)
                hLb = sc.op("sync", lambda e, tsl=tsl: e.dma_start(
                    w_ba[0:32, 0, :], bas_hf[:, 0:2, tsl]),
                    deps=st["bacons"], dma=True)
                hLa = sc.op("sync", lambda e, tsl=tsl: e.dma_start(
                    w_ba[0:32, 1, :], bas_hf[:, 2:4, tsl]),
                    dma=True)
                hLq = hLz = None
                if outc:
                    hLq = sc.op("sync", lambda e, tsl=tsl: e.dma_start(
                        w_q[:, :, :], qs_v[:, :, tsl]), deps=st["qcons"], dma=True)
                    hLz = sc.op("sync", lambda e, tsl=tsl: e.dma_start(
                        w_z[:, :, :], zs_v[:, :, tsl]), deps=st["zcons"], dma=True)

                R = lambda i: w_rows[0:32, i, :]
                rdeps = st["rowscons"]
                sc.op("vector", lambda e: e.tensor_scalar(R(10), w_ba[0:32, 1, :],
                      sb_dtb[0:32, :], None, ALU.add), deps=[hLa] + rdeps)
                sc.op("vector", lambda e: e.tensor_single_scalar(R(11), R(10), 0.0, ALU.max))
                sc.op("scalar", lambda e: e.activation(R(10), R(10), AF.Abs),
                      deps=[sc.barrier_handle("vector")] + rdeps)
                sc.op("scalar", lambda e: e.activation(R(10), R(10), AF.Exp, scale=-1.0))
                sc.op("scalar", lambda e: e.activation(R(10), R(10), AF.Ln,
                      bias=sb_eps[0:32, 2:3]))
                sc.op("vector", lambda e: e.tensor_tensor(R(10), R(10), R(11), ALU.add),
                      deps=[sc.barrier_handle("scalar")])
                sc.op("vector", lambda e: e.tensor_scalar(R(0), R(10), sb_negea[0:32, :],
                      None, ALU.mult))
                hG = sc.op("vector", lambda e: e.tensor_tensor_scan(
                    R(1), R(0), sb_z32[0:32, :], 0.0, ALU.add, ALU.add))
                hSg = sc.op("scalar", lambda e: e.activation(R(10), w_ba[0:32, 0, :],
                            AF.Sigmoid), deps=[hLb] + rdeps)
                hLnb = sc.op("scalar", lambda e: e.activation(R(2), R(10), AF.Ln))
                hKsq = sc.op("scalar", lambda e: e.activation(
                    w_sq2.rearrange("p (a b) -> p a b", a=16), w_k[:, :, :], AF.Square),
                    deps=[hLk, st["b7"]])
                hKrow = None
                row1_busy = st.get("row1cons")
                for sl in range(4):
                    hmmk = sc.op("tensor", lambda e, sl=sl: e.matmul(
                        ps[0:1, 7, 0:512], sb_ones1[:, :], w_sq2[:, ts(sl, 512)],
                        start=True, stop=True),
                        deps=[hKsq, st["b7"], hKrow])
                    hKrow = sc.op("vector", lambda e, sl=sl: e.tensor_copy(
                        w_row1[0:1, ts(sl, 512)], ps[0:1, 7, 0:512]),
                        deps=[hmmk, row1_busy])
                st["b7"] = hKrow
                hKst = sc.op("sync", lambda e: e.dma_start(
                    rfl[RFL_KSQ:RFL_KSQ + 16, :], w_row1[0:1, :]), deps=[hKrow], dma=True)
                hKld = sc.op("sync", lambda e: e.dma_start(
                    w_sqr[0:32, 0, :],
                    rfl[RFL_KSQ:RFL_KSQ + 16, :].unsqueeze(1).broadcast_to([16, 2, C])),
                    deps=[hKst], dma=True)
                hLnk = sc.op("scalar", lambda e: e.activation(
                    R(3), w_sqr[0:32, 0, :], AF.Ln, bias=sb_eps[0:32, 1:2]), deps=[hKld])
                hLnq = None
                if outc:
                    hQsq = sc.op("scalar", lambda e: e.activation(
                        w_sq2.rearrange("p (a b) -> p a b", a=16), w_q[:, :, :], AF.Square),
                        deps=[hLq, hKrow])
                    hQrow = None
                    for sl in range(4):
                        hmmq = sc.op("tensor", lambda e, sl=sl: e.matmul(
                            ps[0:1, 7, 0:512], sb_ones1[:, :], w_sq2[:, ts(sl, 512)],
                            start=True, stop=True),
                            deps=[hQsq, st["b7"], hQrow])
                        hQrow = sc.op("vector", lambda e, sl=sl: e.tensor_copy(
                            w_row1[0:1, ts(sl, 512)], ps[0:1, 7, 0:512]), deps=[hmmq])
                    st["b7"] = hQrow
                    hQst = sc.op("sync", lambda e: e.dma_start(
                        rfl[RFL_QSQ:RFL_QSQ + 16, :], w_row1[0:1, :]), deps=[hQrow], dma=True)
                    hQld = sc.op("sync", lambda e: e.dma_start(
                        w_sqr[0:32, 1, :],
                        rfl[RFL_QSQ:RFL_QSQ + 16, :].unsqueeze(1).broadcast_to([16, 2, C])),
                        deps=[hQst], dma=True)
                    hLnq = sc.op("scalar", lambda e: e.activation(
                        R(4), w_sqr[0:32, 1, :], AF.Ln, bias=sb_eps[0:32, 1:2]), deps=[hQld])
                hPa = sc.op("vector", lambda e: e.scalar_tensor_tensor(
                    R(5), R(3), -0.5, R(2), ALU.mult, ALU.add),
                    deps=[hLnk, hLnb, hG])
                hPa = sc.op("vector", lambda e: e.tensor_tensor(R(5), R(5), R(1), ALU.subtract))
                hFa = sc.op("vector", lambda e: e.scalar_tensor_tensor(
                    R(6), R(3), -0.5, R(1), ALU.mult, ALU.add))
                hsa = sc.op("vector", lambda e: e.tensor_tensor(R(10), R(6), R(2), ALU.add))
                hesa = sc.op("scalar", lambda e: e.activation(R(8), R(10), AF.Exp),
                             deps=[hsa])
                hFe = heqe = None
                if outc:
                    hFe = sc.op("vector", lambda e: e.scalar_tensor_tensor(
                        R(7), R(4), -0.5, R(1), ALU.mult, ALU.add), deps=[hLnq])
                    hFe = sc.op("vector", lambda e: e.tensor_scalar(
                        R(7), R(7), LNHALF_DK, None, ALU.add))
                    heqe = sc.op("scalar", lambda e: e.activation(R(9), R(7), AF.Exp),
                                 deps=[hFe])
                hsk2 = sc.op("vector", lambda e: e.tensor_scalar(
                    R(12), R(5), R(1)[:, 127:128], None, ALU.add), deps=[hPa])
                hesk = sc.op("scalar", lambda e: e.activation(R(13), R(12), AF.Exp),
                             deps=[hsk2])
                hlC = sc.op("scalar", lambda e: e.activation(
                    R(10)[:, 0:1], R(1)[:, 127:128], AF.Exp), deps=[hG, hsa])
                hB1 = sc.op("sync", lambda e: e.dma_start(
                    rfl[RFL_PA:RFL_PA + 96, :].rearrange("(s l) t -> l s t", s=3),
                    w_rows[0:32, 5:8, :]),
                    deps=[hPa, hFa, hFe, (hFe if outc else hFa),
                          st["rflsync"], st["rflgp"]], dma=True)
                hB2 = sc.op("sync", lambda e: e.dma_start(
                    rfl[RFL_ESA:RFL_ESA + 64, :].rearrange("(s l) t -> l s t", s=2),
                    w_rows[0:32, 8:10, :]),
                    deps=[hesa, (heqe if outc else hesa), st["rflgp"]], dma=True)
                hB3 = sc.op("sync", lambda e: e.dma_start(
                    rfl[RFL_LC:RFL_LC + 32, 0:1], R(10)[:, 0:1],
                    allow_slow_non_contiguous=True),
                    deps=[hlC, st["lcfree"]], dma=True)
                hlcl = sc.op("sync", lambda e: e.dma_start(
                    w_row1[0:1, 0:32], rfl[RFL_LC:RFL_LC + 32, 0:1],
                    allow_slow_non_contiguous=True), deps=[hB3], dma=True)
                st["lcfree"] = hlcl
                def kkmms(src2):
                    def f(e):
                        last = None
                        for h in range(16):
                            last = e.matmul(ps[:, h // 4, ts(h % 4)], w_k[:, h, :],
                                            src2[:, h, :], start=True, stop=True, skip_group_check=True)
                        return last
                    return f
                hKK = sc.op("tensor", kkmms(w_k), deps=[hLk, st["b03"]])
                pvKK3 = ps[:, 0:4, :].rearrange("p a b -> p (a b)").rearrange(
                    "p (a b) -> p a b", a=16)
                mview = lambda i: sb_masks[:, i, :].unsqueeze(1).broadcast_to([128, 16, 128])
                hKKs = sc.op("vector", lambda e: e.tensor_tensor(
                    w_KKs[:, :, :], pvKK3, mview(0), ALU.mult),
                    deps=[hKK, st["kkfree"]])
                hKKl = sc.op("vector", lambda e: e.tensor_tensor(
                    w_KKl[:, :, :], pvKK3, mview(2), ALU.mult))
                hKQe = None
                if outc:
                    hKQ = sc.op("tensor", kkmms(w_q), deps=[hLq, hKKl])
                    hKQe = sc.op("vector", lambda e: e.tensor_tensor(
                        w_KQ[:, :, :], pvKK3, mview(1), ALU.mult),
                        deps=[hKQ])
                st["b03"] = hKQe if outc else hKKl
                if ch == 0 and "sRFL2" in dbg_outs:
                    st["gfree"] = st["gfree"] + [sc.op("sync", lambda e: e.dma_start(
                        dbg_outs["sRFL2"], rfl[RFL_ESA:RFL_ESA + 64, :]),
                        deps=[hB2], dma=True)]
                if ch == 0 and "sRFL1" in dbg_outs:
                    st["gfree"] = st["gfree"] + [sc.op("sync", lambda e: e.dma_start(
                        dbg_outs["sRFL1"], rfl[RFL_PA:RFL_PA + 96, :]),
                        deps=[hB1], dma=True)]
                if ch == 0 and "sKK" in dbg_outs:
                    st["gfree"] = st["gfree"] + [sc.op("gpsimd", lambda e: e.dma_start(
                        dbg_outs["sKK"],
                        w_KKs.rearrange("p a b -> p (a b)")), deps=[hKKs], dma=True)]
                hTs = {}
                for slot, (ridx, hdep) in enumerate([(5, hPa), (6, hFa), (13, hesk)]):
                    htp = sc.op("tensor", lambda e, ridx=ridx: e.transpose(
                        ps[:, 6, 0:32], w_rows[0:32, ridx, :], sb_identf[0:32, 0:32]),
                        deps=[hdep, st["b6"]])
                    hTs[slot] = sc.op("scalar", lambda e, slot=slot: e.copy(
                        w_PT[:, slot, :], ps[:, 6, 0:32]), deps=[htp, st["ptfree"]])
                    st["b6"] = hTs[slot]
                hlmm = sc.op("tensor", lambda e: e.matmul(
                    ps[:, 6, 0:32], sb_onesrf[:, :], w_row1[0:1, 0:32],
                    start=True, stop=True), deps=[hlcl, st["b6"], st["supd"]])
                hlbc = sc.op("scalar", lambda e: e.copy(w_lbc[:, :], ps[:, 6, 0:32]),
                             deps=[hlmm])
                st["b6"] = hlbc

                newgfree = []
                hzs = hqp = None
                for g in range(2):
                    L0 = 16 * g
                    KH = slice(8 * g, 8 * g + 8)
                    gfree = st["gfree"]
                    lane_kk = lambda t, KH=KH: t[:, KH, :].unsqueeze(2) \
                        .broadcast_to([128, 8, 2, 128])
                    def build_tile(flat_row0, pt_slot, kkmasked, dst, hrowdeps, freedep,
                                   name=""):
                        hdup = sc.op("sync", lambda e, flat_row0=flat_row0, L0=L0: e.dma_start(
                            w_F2[:, :],
                            rfl[flat_row0 + L0:flat_row0 + L0 + 16, :]
                            .rearrange("a b -> (a b)").unsqueeze(0)
                            .broadcast_to([128, 16 * C])),
                            deps=hrowdeps + gfree + [st["f2free"]], dma=True)
                        st["rflsync"] = hdup
                        hts = None
                        for li in range(16):
                            hts = sc.op("gpsimd", lambda e, li=li, pt_slot=pt_slot, L0=L0: e.tensor_scalar(
                                w_ARG[:, ts(li)], w_F2[:, ts(li)],
                                w_PT[:, pt_slot, L0 + li:L0 + li + 1], CLAMP,
                                ALU.add, ALU.min),
                                deps=[hdup, hTs[pt_slot], st["argfree"]] + gfree)
                        st["f2free"] = hts
                        hexp = sc.op("scalar", lambda e, dst=dst: e.activation(
                            dst, w_ARG[:, :], AF.Exp), deps=[hts, freedep])
                        st["argfree"] = hexp
                        htap = None
                        tapk = "sEXP" + name
                        if ch == 0 and g == 0 and tapk in dbg_outs:
                            htap = sc.op("gpsimd", lambda e, tapk=tapk, dst=dst:
                                e.dma_start(dbg_outs[tapk], dst),
                                deps=[hexp], dma=True)
                            st["gfree"] = st["gfree"] + [htap]
                        d4 = dst.rearrange("p (a b c) -> p a b c", a=8, b=2)
                        hmul = sc.op("gpsimd", lambda e, d4=d4, kkmasked=kkmasked,
                                     lane_kk=lane_kk: e.tensor_tensor(
                            d4, d4, lane_kk(kkmasked), ALU.mult),
                            deps=[hexp, htap, (hKKs if kkmasked is w_KKs else
                                         (hKKl if kkmasked is w_KKl else hKQe))])
                        return hmul
                    hA = build_tile(RFL_FA, 0, w_KKs, w_AT[:, :], [hB1],
                                    st["atfree"], "A")
                    hBt = build_tile(RFL_PA, 1, w_KKl, w_B0[:, :], [hB1],
                                     st["b0free"], "B")
                    hE = None
                    if outc:
                        hE = build_tile(RFL_FE, 0, w_KQ, w_ET[:, :], [hB1],
                                        st["etfree"], "E")
                    hdsa = sc.op("gpsimd", lambda e, L0=L0: e.dma_start(
                        w_dupb[:, :],
                        rfl[RFL_ESA + L0:RFL_ESA + L0 + 16, :]
                        .rearrange("a b -> (a b)").unsqueeze(0)
                        .broadcast_to([128, 16 * C])),
                        deps=[hB2, st["dupbfree"]] + gfree, dma=True)
                    st["rflgp"] = hdsa
                    hdtap = None
                    if ch == 0 and g == 0 and "sDup" in dbg_outs:
                        hdtap = sc.op("gpsimd", lambda e: e.dma_start(
                            dbg_outs["sDup"], w_dupb[:, :]), deps=[hdsa], dma=True)
                    hkp = sc.op("gpsimd", lambda e, lane_kk=lane_kk: e.tensor_tensor(
                        w_kp.rearrange("p (a b c) -> p a b c", a=8, b=2),
                        w_dupb.rearrange("p (a b c) -> p a b c", a=8, b=2),
                        lane_kk(w_k), ALU.mult),
                        deps=[hdsa, hLk, st["kpfree"], hdtap])
                    hqp = None
                    if outc:
                        hdqe = sc.op("gpsimd", lambda e, L0=L0: e.dma_start(
                            w_dupb[:, :],
                            rfl[RFL_EQE + L0:RFL_EQE + L0 + 16, :]
                            .rearrange("a b -> (a b)").unsqueeze(0)
                            .broadcast_to([128, 16 * C])),
                            deps=[hB2, hkp], dma=True)
                        st["rflgp"] = hdqe
                        hqp = sc.op("gpsimd", lambda e, lane_kk=lane_kk: e.tensor_tensor(
                            w_qp.rearrange("p (a b c) -> p a b c", a=8, b=2),
                            w_dupb.rearrange("p (a b c) -> p a b c", a=8, b=2),
                            lane_kk(w_q), ALU.mult),
                            deps=[hdqe, hLq, st["qpfree"]])
                        st["dupbfree"] = hqp
                    else:
                        st["dupbfree"] = hkp
                    ktv = w_kt.rearrange("p (a c) -> p a c", a=16)
                    hkps = None
                    for li in range(16):
                        hkps = sc.op("gpsimd", lambda e, li=li, L0=L0, ktv=ktv: e.tensor_scalar(
                            w_kpp[:, ts(li)], ktv[:, (L0 + li) // 2, :],
                            w_PT[:, 2, L0 + li:L0 + li + 1], None, ALU.mult),
                            deps=[hLkt, hTs[2], st["supd"]])
                    hRm = sc.op("tensor", lambda e, L0=L0: (
                        [e.matmul(ps[:, li // 4, ts(li % 4)], w_kp[:, ts(li)],
                                  sb_Sb[:, L0 + li, :], start=True, stop=True, skip_group_check=True)
                         for li in range(16)][-1]),
                        deps=[hkp, st["b03"], st["supd"]])
                    st["kpfree"] = hRm
                    pv03 = ps[:, 0:4, :].rearrange("p a b -> p (a b)")
                    hRev = sc.op("vector", lambda e, g=g: e.tensor_tensor(
                        w_X[:, :], w_vt[:, ts(g, 2048)], pv03, ALU.subtract),
                        deps=[hRm, hLvt])
                    st["b03"] = hRev
                    hCev = {0: hA}
                    hBev = {0: hBt}
                    cb03 = st["b03"]
                    cb47 = st["b47"]
                    for k in range(1, 7):
                        Cp = w_AT[:, :] if k == 1 else w_pw[:, k - 2, :]
                        Bp = w_B0[:, :] if k == 1 else w_pb[:, (k - 1) % 2, :]
                        hCm = sc.op("tensor", lambda e, Cp=Cp, Bp=Bp: (
                            [e.matmul(ps[:, li // 4, ts(li % 4)], Bp[:, ts(li)],
                                      Cp[:, ts(li)], start=True, stop=True, skip_group_check=True)
                             for li in range(16)][-1]),
                            deps=[hCev[k - 1], hBev[k - 1], cb03])
                        hCev[k] = sc.op("vector", lambda e, k=k: e.tensor_copy(
                            w_pw[:, k - 1, :], pv03), deps=[hCm])
                        cb03 = hCev[k]
                        if k < 6:
                            hBm = sc.op("tensor", lambda e, Cp=Cp, Bp=Bp: (
                                [e.matmul(ps[:, 4 + li // 4, ts(li % 4)], Cp[:, ts(li)],
                                          Bp[:, ts(li)], start=True, stop=True, skip_group_check=True)
                                 for li in range(16)][-1]),
                                deps=[hCev[k - 1], hBev[k - 1], cb47])
                            hBev[k] = sc.op("scalar", lambda e, k=k: e.copy(
                                w_pb[:, k % 2, :],
                                ps[:, 4:8, :].rearrange("p a b -> p (a b)")), deps=[hBm])
                            cb47 = hBev[k]
                    st["b47"] = cb47
                    st["b0free"] = hBev[1]
                    happ = hRev
                    for ai, k in enumerate([6, 5, 4, 3, 2, 1, 0]):
                        Ck = w_AT[:, :] if k == 0 else w_pw[:, k - 1, :]
                        hXm = sc.op("tensor", lambda e, Ck=Ck: (
                            [e.matmul(ps[:, li // 4, ts(li % 4)], Ck[:, ts(li)],
                                      w_X[:, ts(li)], start=True, stop=True, skip_group_check=True)
                             for li in range(16)][-1]),
                            deps=[happ, cb03, (hCev[k] if k > 0 else hA)])
                        if k > 0:
                            happ = sc.op("vector", lambda e: e.tensor_tensor(
                                w_X[:, :], w_X[:, :], pv03, ALU.add), deps=[hXm])
                        else:
                            happ = sc.op("vector", lambda e: e.scalar_tensor_tensor(
                                w_X[:, :], pv03, -1.0, w_X[:, :], ALU.mult, ALU.add),
                                deps=[hXm])
                        cb03 = happ
                    st["b03"] = happ
                    st["atfree"] = happ
                    hy = None
                    if outc:
                        hOm = sc.op("tensor", lambda e, L0=L0: (
                            [(e.matmul(ps[:, li // 4, ts(li % 4)], sb_Sb[:, L0 + li, :],
                                       w_qp[:, ts(li)], start=True, stop=False, skip_group_check=True),
                              e.matmul(ps[:, li // 4, ts(li % 4)], w_X[:, ts(li)],
                                       w_ET[:, ts(li)], start=False, stop=True, skip_group_check=True))[-1]
                             for li in range(16)][-1]),
                            deps=[happ, hqp, hE, st["b03"], st["supd"]])
                        st["etfree"] = hOm
                        st["qpfree"] = hOm
                        hzs = sc.op("scalar", lambda e, L0=L0: e.activation(
                            w_zsil[:, :], w_z[:, L0:L0 + 16, :], AF.Silu),
                            deps=[hLz, st["zsilfree"]])
                        hxg = sc.op("vector", lambda e: e.tensor_tensor(
                            w_xg[:, :], pv03, w_zsil[:, :], ALU.mult), deps=[hOm, hzs])
                        st["b03"] = hxg
                        st["zsilfree"] = hxg
                        hgq = sc.op("scalar", lambda e: e.activation(
                            w_gsq.rearrange("p (a b) -> p a b", a=16), 
                            w_xg.rearrange("p (a b) -> p a b", a=16), AF.Square),
                            deps=[hxg])
                        hrr = None
                        for sl in range(4):
                            hrm = sc.op("tensor", lambda e, sl=sl: e.matmul(
                                ps[0:1, 7, 0:512], sb_ones1[:, :], w_gsq[:, ts(sl, 512)],
                                start=True, stop=True),
                                deps=[hgq, st["b7"], hrr, hlmm, hQld])
                            hrr = sc.op("vector", lambda e, sl=sl: e.tensor_copy(
                                w_row1[0:1, ts(sl, 512)], ps[0:1, 7, 0:512]), deps=[hrm])
                        st["b7"] = hrr
                        hrl = sc.op("scalar", lambda e: e.activation(
                            w_row1[0:1, :], w_row1[0:1, :], AF.Ln, scale=1.0 / 128,
                            bias=sb_eps[0:1, 1:2]), deps=[hrr])
                        hre = sc.op("scalar", lambda e: e.activation(
                            w_row1[0:1, :], w_row1[0:1, :], AF.Exp, scale=-0.5))
                        hrc = sc.op("vector", lambda e: e.tensor_copy(
                            w_rowb[0:1, :], w_row1[0:1, :]), deps=[hre])
                        st["row1cons"] = hrc
                        hbcm = sc.op("tensor", lambda e: (
                            [e.matmul(ps[:, 4 + sl, 0:512], sb_onesr[:, :],
                                      w_rowb[0:1, ts(sl, 512)], start=True, stop=True, skip_group_check=True)
                             for sl in range(4)][-1]),
                            deps=[hrc, st["b47"]])
                        pv47 = ps[:, 4:8, :].rearrange("p a b -> p (a b)")
                        hy = sc.op("vector", lambda e: e.scalar_tensor_tensor(
                            w_y[:, :], w_xg[:, :], sb_anwc[:, :], pv47,
                            ALU.mult, ALU.mult), deps=[hbcm, st["ycons"]])
                        st["b47"] = hy
                        hyst = sc.op("sync", lambda e, L0=L0, tsl=tsl: e.dma_start(
                            ys_v[:, L0:L0 + 16, tsl], w_y.rearrange("p (a b) -> p a b", a=16)),
                            deps=[hy], dma=True)
                        newgfree.append(hyst)
                        st["ycons"] = hyst
                    hSm = sc.op("tensor", lambda e: (
                        [e.matmul(ps[:, li // 4, ts(li % 4)], w_kpp[:, ts(li)],
                                  w_X[:, ts(li)], start=True, stop=True, skip_group_check=True)
                         for li in range(16)][-1]),
                        deps=[hkps, happ, st["b03"], ((hy if outc else None))])
                    hSe = None
                    for li in range(16):
                        hSe = sc.op("vector", lambda e, li=li, L0=L0: e.scalar_tensor_tensor(
                            sb_S[:, L0 + li, :], sb_S[:, L0 + li, :],
                            w_lbc[:, L0 + li:L0 + li + 1],
                            ps[:, li // 4, ts(li % 4)], ALU.mult, ALU.add),
                            deps=[hSm, hlbc])
                    hSb = sc.op("vector", lambda e, L0=L0: e.tensor_copy(
                        sb_Sb[:, L0:L0 + 16, :], sb_S[:, L0:L0 + 16, :]), deps=[hSe])
                    st["b03"] = hSb
                    st["supd"] = hSb
                    newgfree.extend([hSb, happ])
                    if ch == 0 and g == 0:
                        tap0 = len(newgfree)
                        if "sX" in dbg_outs:
                            newgfree.append(sc.op("gpsimd", lambda e: e.dma_start(
                                dbg_outs["sX"], w_X[:, :]), deps=[happ], dma=True))
                        if "sA" in dbg_outs:
                            newgfree.append(sc.op("gpsimd", lambda e: e.dma_start(
                                dbg_outs["sA"], w_AT[:, :]), deps=[hA], dma=True))
                        if "sKp" in dbg_outs:
                            newgfree.append(sc.op("gpsimd", lambda e: e.dma_start(
                                dbg_outs["sKp"], w_kp[:, :]), deps=[hkp], dma=True))
                        if "sKpp" in dbg_outs:
                            newgfree.append(sc.op("gpsimd", lambda e: e.dma_start(
                                dbg_outs["sKpp"], w_kpp[:, :]), deps=[hkps], dma=True))
                        if "sBAS" in dbg_outs:
                            newgfree.append(sc.op("sync", lambda e: e.dma_start(
                                dbg_outs["sBAS"], bas[:, 0:128]),
                                deps=[hLb, hLa], dma=True))
                        if "sBA" in dbg_outs:
                            newgfree.append(sc.op("sync", lambda e: e.dma_start(
                                dbg_outs["sBA"][0:32, :],
                                w_ba[0:32, :, :].rearrange("p a b -> p (a b)")),
                                deps=[hLb, hLa], dma=True))
                        if "sRows" in dbg_outs:
                            newgfree.append(sc.op("sync", lambda e: e.dma_start(
                                dbg_outs["sRows"][0:32, :],
                                w_rows[0:32, :, :].rearrange("p a b -> p (a b)")),
                                deps=[hesk, hB1, hB2], dma=True))
                        if outc:
                            if "sET" in dbg_outs:
                                newgfree.append(sc.op("gpsimd", lambda e: e.dma_start(
                                    dbg_outs["sET"], w_ET[:, :]), deps=[hy], dma=True))
                            if "sZsil" in dbg_outs:
                                newgfree.append(sc.op("gpsimd", lambda e: e.dma_start(
                                    dbg_outs["sZsil"], w_zsil[:, :]), deps=[hy], dma=True))
                            if "sQp" in dbg_outs:
                                newgfree.append(sc.op("gpsimd", lambda e: e.dma_start(
                                    dbg_outs["sQp"], w_qp[:, :]), deps=[hy], dma=True))
                            if "sXg" in dbg_outs:
                                newgfree.append(sc.op("gpsimd", lambda e: e.dma_start(
                                    dbg_outs["sXg"], w_xg[:, :]), deps=[hy], dma=True))
                            if "sY" in dbg_outs:
                                newgfree.append(sc.op("gpsimd", lambda e: e.dma_start(
                                    dbg_outs["sY"], w_y[:, :]), deps=[hy], dma=True))
                            if "sRow1" in dbg_outs:
                                newgfree.append(sc.op("gpsimd", lambda e: e.dma_start(
                                    dbg_outs["sRow1"], w_row1[0:1, :]), deps=[hy], dma=True))
                        tapsh = newgfree[tap0:]
                        if tapsh:
                            hbar = sc.op("gpsimd", lambda e: e.memset(
                                sb_iota[0:1, 0:1], 0.0),
                                deps=tapsh + [st["supd"], st["atfree"],
                                              st["b0free"], st["kpfree"],
                                              st["b7"], st["etfree"],
                                              st["qpfree"], st["zsilfree"]])
                            st["supd"] = hbar
                            st["atfree"] = hbar
                            st["b0free"] = hbar
                            st["kpfree"] = hbar
                            st["argfree"] = hbar
                            st["dupbfree"] = hbar
                            st["b7"] = hbar
                            st["etfree"] = hbar
                            st["qpfree"] = hbar
                            st["zsilfree"] = hbar
                    tapch = max(int(os.environ.get("TAPSTOP", "0")) - 1, 0)
                    if ch == tapch and g == 1 and "sS" in dbg_outs:
                        newgfree.append(sc.op("sync", lambda e: e.dma_start(
                            dbg_outs["sS"], sb_S[:, :, :].rearrange("p a b -> p (a b)")),
                            deps=[hSb], dma=True))
                st["gfree"] = newgfree
                st["kkfree"] = hE if outc else hBt
                st["ptfree"] = hkps
                st["kcons"] = [hkp, (hKQe if outc else hKKl)]
                st["ktcons"] = [hkps]
                st["vtcons"] = [hRev]
                st["bacons"] = [hLnb, hSg]
                st["qcons"] = ([hqp] if outc else st["qcons"])
                st["zcons"] = ([hzs] if outc else st["zcons"])
                st["rowscons"] = [hTs[0], hTs[1], hTs[2], hB1, hB2, hB3, hesk]

            def scan_phase(front_done):
                fd = list(front_done)
                st = {"kcons": fd, "ktcons": fd, "vtcons": fd, "bacons": fd,
                      "qcons": fd, "zcons": fd, "rowscons": fd, "gfree": fd,
                      "supd": sc.barrier_handle("vector"),
                      "b03": None, "b47": None, "b6": None, "b7": None,
                      "argfree": None, "dupbfree": None, "ycons": None,
                      "row1cons": None, "f2free": None, "rflsync": None,
                      "rflgp": None, "lcfree": None, "kkfree": None,
                      "ptfree": None, "atfree": None, "b0free": None,
                      "etfree": None, "kpfree": None, "qpfree": None,
                      "zsilfree": None}
                sso = [] if role == 0 else list(range(NC0))
                osc = list(range(NC0)) if role == 0 else list(range(NC0, NC0 + NC1))
                tstop = int(os.environ.get("TAPSTOP", "0"))
                casts = [lambda e, ei=ei: e.dma_start(
                    ewb[ei * D:(ei + 1) * D, :], expert_w[ei * D:(ei + 1) * D, :])
                    for ei in range(NE)]
                casts.append(lambda e: e.dma_start(pwb, proj_w))
                if tstop > 0:
                    for chn in range(tstop):
                        scan_chunk(sc, chn, True, st)
                    return st
                nch = 0
                for chn in sso:
                    scan_chunk(sc, chn, False, st)
                    nch += 1
                    if nch % 3 == 0 and casts:
                        sc.op("gpsimd", casts.pop(0), dma=True)
                for chn in osc:
                    scan_chunk(sc, chn, True, st)
                    nch += 1
                    if nch % 3 == 0 and casts:
                        sc.op("gpsimd", casts.pop(0), dma=True)
                for fn in casts:
                    sc.op("gpsimd", fn, dma=True)
                return st

            def epilogue_phase(all_done):
                t0r = 0 if role == 0 else P
                ngrp = NC0 if role == 0 else NC1
                pv03 = ps[:, 0:4, :].rearrange("p a b -> p (a b)")
                mcol = lambda i: e1_st[:, i:i + 1]

                hwo = None
                for kt in range(32):
                    hwo = sc.op("gpsimd", lambda e, kt=kt: e.dma_start(
                        e1_wo[:, kt, :], wo_v2[kt]), deps=all_done, dma=True)
                hrw = sc.op("gpsimd", lambda e: e.dma_start(e1_rw[:, :, :], rw_v),
                            deps=all_done, dma=True)
                hrb = sc.op("gpsimd", lambda e: e.dma_start(
                    e1_rb[0:1, :], router_b), deps=all_done, dma=True)

                ngrp1 = int(os.environ.get("E1N", "0")) or ngrp
                st1 = {"ysfree": all_done, "xbfree": all_done, "b03": None,
                       "b45": None, "b6": None, "l2Tcons": [], "x1free": [],
                       "l2bfree": []}
                for m in range(ngrp1):
                    gt0 = t0r + m * 128
                    lt0 = m * 128
                    hys = sc.op("sync", lambda e, gt0=gt0: e.dma_start(
                        e1_ys[:, :, :], ys_v[:, :, gt0:gt0 + 128]),
                        deps=st1["ysfree"], dma=True)
                    hxb = sc.op("sync", lambda e, gt0=gt0: e.dma_start(
                        e1_xb[:, :], xbt[gt0:gt0 + 128, :]),
                        deps=list(st1["xbfree"]), dma=True)
                    hmm = sc.op("tensor", lambda e: (
                        [[e.matmul(ps[:, sl, :], e1_ys[:, h, :],
                                   e1_wo[:, h, ts(sl, 512)],
                                   start=(h == 0), stop=(h == 31),
                                   skip_group_check=True)
                          for h in range(32)][-1] for sl in range(4)][-1]),
                        deps=[hys, hwo, st1["b03"]])
                    hevc = sc.op("scalar", lambda e: e.copy(e1_x1[:, :], pv03),
                                 deps=[hmm] + st1["x1free"])
                    st1["b03"] = hevc
                    hsum = sc.op("vector", lambda e: e.tensor_reduce(
                        mcol(0), e1_x1[:, :], AX.X, ALU.add), deps=[hevc])
                    hsq = sc.op("scalar", lambda e: e.activation(
                        e1_sq2[:, :], e1_x1[:, :], AF.Square, accum_out=mcol(1)),
                        deps=[hevc, hmm])
                    hmean = sc.op("vector", lambda e: e.tensor_scalar(
                        mcol(2), mcol(0), 1.0 / D, None, ALU.mult), deps=[hsum])
                    hmsq = sc.op("vector", lambda e: e.tensor_scalar(
                        mcol(3), mcol(2), mcol(2), None, ALU.mult))
                    ht1 = sc.op("vector", lambda e: e.tensor_scalar(
                        mcol(4), mcol(1), 1.0 / D, None, ALU.mult), deps=[hsq])
                    hvar = sc.op("vector", lambda e: e.tensor_tensor(
                        mcol(4), mcol(4), mcol(3), ALU.subtract))
                    hrs = sc.op("scalar", lambda e: e.activation(
                        mcol(5), mcol(4), AF.Ln, bias=sb_eps[:, 0:1]),
                        deps=[hvar])
                    hrs = sc.op("scalar", lambda e: e.activation(
                        mcol(5), mcol(5), AF.Exp, scale=-0.5))
                    hx1 = sc.op("vector", lambda e: e.tensor_scalar(
                        e1_x1[:, :], e1_x1[:, :], mcol(2), mcol(5),
                        ALU.subtract, ALU.mult), deps=[hrs, hsq])
                    hx1 = sc.op("vector", lambda e: e.tensor_tensor(
                        e1_x1[:, :], e1_x1[:, :], e1_xb[:, :], ALU.add),
                        deps=[hxb])
                    st1["xbfree"] = [hx1]
                    hx1st = sc.op("sync", lambda e, lt0=lt0: e.dma_start(
                        x1s[lt0:lt0 + 128, :], e1_x1[:, :]), deps=[hx1], dma=True)
                    hsum2 = sc.op("vector", lambda e: e.tensor_reduce(
                        mcol(0), e1_x1[:, :], AX.X, ALU.add), deps=[hx1])
                    hsq2 = sc.op("scalar", lambda e: e.activation(
                        e1_sq2[:, :], e1_x1[:, :], AF.Square, accum_out=mcol(1)),
                        deps=[hx1, hmm])
                    hmean2 = sc.op("vector", lambda e: e.tensor_scalar(
                        mcol(2), mcol(0), 1.0 / D, None, ALU.mult), deps=[hsum2])
                    hmsq2 = sc.op("vector", lambda e: e.tensor_scalar(
                        mcol(3), mcol(2), mcol(2), None, ALU.mult))
                    ht12 = sc.op("vector", lambda e: e.tensor_scalar(
                        mcol(4), mcol(1), 1.0 / D, None, ALU.mult), deps=[hsq2])
                    hvar2 = sc.op("vector", lambda e: e.tensor_tensor(
                        mcol(4), mcol(4), mcol(3), ALU.subtract))
                    hrs2 = sc.op("scalar", lambda e: e.activation(
                        mcol(5), mcol(4), AF.Ln, bias=sb_eps[:, 0:1]),
                        deps=[hvar2])
                    hrs2 = sc.op("scalar", lambda e: e.activation(
                        mcol(5), mcol(5), AF.Exp, scale=-0.5))
                    hl2 = sc.op("vector", lambda e: e.tensor_scalar(
                        e1_l2b[:, :], e1_x1[:, :], mcol(2), mcol(5),
                        ALU.subtract, ALU.mult), deps=[hrs2] + st1["l2bfree"])
                    st1["x1free"] = [hl2, hx1st]
                    hl2st = sc.op("sync", lambda e, lt0=lt0: e.dma_start(
                        ln2s[lt0:lt0 + 128, :], e1_l2b[:, :]), deps=[hl2],
                        dma=True)
                    htr = sc.op("tensor", lambda e: (
                        [e.transpose(psb[:, 4 + kt // 8, (kt % 8) * 128:
                                         (kt % 8) * 128 + 128],
                                     e1_l2b[:, ts(kt)], sb_ident[:, :])
                         for kt in range(16)][-1]),
                        deps=[hl2, st1["b45"]])
                    hev = sc.op("scalar", lambda e: e.copy(
                        e1_l2T.rearrange("p a b -> p (a b)"),
                        psb[:, 4:6, :].rearrange("p a b -> p (a b)")),
                        deps=[htr] + st1["l2Tcons"])
                    st1["b45"] = hev
                    st1["l2bfree"] = [htr, hl2st]
                    hl2Tst = sc.op("sync", lambda e, lt0=lt0: e.dma_start(
                        l2ts_v[:, :, lt0:lt0 + 128], e1_l2T[:, :, :]),
                        deps=[hev], dma=True)
                    hrm = sc.op("tensor", lambda e: (
                        [e.matmul(ps[:, 6, 0:8], e1_l2T[:, kt, :],
                                  e1_rw[:, kt, :], start=(kt == 0), stop=False,
                                  skip_group_check=True)
                         for kt in range(16)][-1],
                        e.matmul(ps[:, 6, 0:8], sb_onesr[:, :], e1_rb[0:1, :],
                                 start=False, stop=True, skip_group_check=True))[-1],
                        deps=[hev, hrw, hrb, st1["b6"]])
                    hlg = sc.op("vector", lambda e: e.tensor_copy(
                        e1_lg[:, :], ps[:, 6, 0:8]), deps=[hrm])
                    st1["b6"] = hlg
                    hm1 = sc.op("vector", lambda e: e.tensor_reduce(
                        mcol(6), e1_lg[:, :], AX.X, ALU.max))
                    hk1 = sc.op("vector", lambda e: e.tensor_scalar(
                        e1_t8a[:, :], e1_lg[:, :], mcol(6), None, ALU.is_equal))
                    hlp = sc.op("vector", lambda e: e.scalar_tensor_tensor(
                        e1_t8b[:, :], e1_t8a[:, :], -1e9, e1_lg[:, :],
                        ALU.mult, ALU.add))
                    hm2 = sc.op("vector", lambda e: e.tensor_reduce(
                        mcol(7), e1_t8b[:, :], AX.X, ALU.max))
                    hsel = sc.op("vector", lambda e: e.tensor_scalar(
                        e1_t8a[:, :], e1_lg[:, :], mcol(7), None, ALU.is_ge))
                    hsh = sc.op("vector", lambda e: e.tensor_scalar(
                        e1_t8b[:, :], e1_lg[:, :], mcol(6), None, ALU.subtract))
                    hex = sc.op("scalar", lambda e: e.activation(
                        e1_t8b[:, :], e1_t8b[:, :], AF.Exp),
                        deps=[sc.barrier_handle("vector")])
                    hwe = sc.op("vector", lambda e: e.tensor_tensor(
                        e1_t8b[:, :], e1_t8b[:, :], e1_t8a[:, :], ALU.mult),
                        deps=[hex])
                    hden = sc.op("vector", lambda e: e.tensor_reduce(
                        mcol(6), e1_t8b[:, :], AX.X, ALU.add))
                    hrc = sc.op("scalar", lambda e: e.activation(
                        mcol(7), mcol(6), AF.Ln), deps=[hden])
                    hrc = sc.op("scalar", lambda e: e.activation(
                        mcol(7), mcol(7), AF.Exp, scale=-1.0))
                    hfw = sc.op("vector", lambda e: e.tensor_scalar(
                        e1_t8b[:, :], e1_t8b[:, :], mcol(7), None, ALU.mult),
                        deps=[hrc])
                    hgst = sc.op("sync", lambda e, lt0=lt0: e.dma_start(
                        gates[lt0:lt0 + 128, :], e1_t8b[:, :]), deps=[hfw],
                        dma=True)
                    st1["ysfree"] = [hmm, hsq2]
                    st1["l2Tcons"] = [hrm, hl2Tst]

                e1_done = [sc.barrier_handle(e) for e in ENGS] + \
                          [sc.barrier_handle("sync", dma=True),
                           sc.barrier_handle("gpsimd", dma=True)]
                if os.environ.get("EPSTOP", "0") == "1":
                    return

                m2c = lambda i: e2_st[:, i:i + 1]
                hfnr = sc.op("sync", lambda e: e.dma_start(
                    e2_fr[0:1, :], fnorm_w), deps=e1_done, dma=True)
                hfnm = sc.op("tensor", lambda e: (
                    [e.matmul(ps[:, sl, :], sb_onesrf[:, :],
                              e2_fr[0:1, ts(sl, 512)], start=True, stop=True,
                              skip_group_check=True) for sl in range(4)][-1]),
                    deps=[hfnr] + e1_done)
                hfnb = sc.op("vector", lambda e: e.tensor_copy(
                    e2_fnb[:, :], pv03), deps=[hfnm])

                sgs = []
                i0 = 0
                while i0 < ngrp:
                    sgs.append(list(range(i0, min(i0 + GSG, ngrp))))
                    i0 += GSG
                st2 = {"l2Tfree": e1_done, "combfree": [hfnb], "b03": hfnb,
                       "b45": None, "hfree": None,
                       "wfree": [list(e1_done), list(e1_done)],
                       "ebfree": list(e1_done), "gtfree": e1_done,
                       "xfree": e1_done[0], "x1free": e1_done[0],
                       "frfree": [hfnb], "cTfree": None, "yfree": None}
                for sg in sgs:
                    G = len(sg)
                    hl2L, hgtL, hcz = {}, {}, {}
                    for gi, m in enumerate(sg):
                        lt0 = m * 128
                        hl2L[gi] = sc.op("sync", lambda e, gi=gi, lt0=lt0: e.dma_start(
                            e2_l2T[:, gi, :, :], l2ts_v[:, :, lt0:lt0 + 128]),
                            deps=st2["l2Tfree"], dma=True)
                        hgtL[gi] = sc.op("sync", lambda e, gi=gi, lt0=lt0: e.dma_start(
                            e2_gt[:, gi, :], gates[lt0:lt0 + 128, :]),
                            deps=st2["gtfree"], dma=True)
                        hcz[gi] = sc.op("vector", lambda e, gi=gi: e.memset(
                            e2_comb[:, gi, :], 0.0), deps=st2["combfree"])
                    hacc = {gi: hcz[gi] for gi in range(G)}
                    for ei in range(NE):
                        heb = sc.op("gpsimd", lambda e, ei=ei: e.dma_start(
                            e2_eb[0:1, :], expert_b[ei:ei + 1, :]),
                            deps=st2["ebfree"], dma=True)
                        hw = {}
                        for hf in range(2):
                            hw[hf] = sc.op("sync", lambda e, ei=ei, hf=hf: e.dma_start(
                                e2_w[:, hf, :, :], we_v2[ei, hf]),
                                deps=st2["wfree"][hf], dma=True)
                        for gi in range(G):
                            hm = sc.op("tensor", lambda e, gi=gi: (
                                [[e.matmul(ps[:, sl, :], e2_l2T[:, gi, kt, :],
                                           e2_w[:, kt // 8, kt % 8, ts(sl, 512)],
                                           start=(kt == 0), stop=False,
                                           skip_group_check=True)
                                  for kt in range(16)][-1] for sl in range(4)][-1],
                                [e.matmul(ps[:, sl, :], sb_onesr[:, :],
                                          e2_eb[0:1, ts(sl, 512)], start=False,
                                          stop=True, skip_group_check=True)
                                 for sl in range(4)][-1])[-1],
                                deps=[hw[0], hw[1], heb, hl2L[gi], st2["b03"]])
                            hsl = sc.op("scalar", lambda e: e.activation(
                                e2_h[:, :], pv03, AF.Silu),
                                deps=[hm, st2["hfree"]])
                            hac = sc.op("vector", lambda e, gi=gi, ei=ei:
                                e.scalar_tensor_tensor(
                                    e2_comb[:, gi, :], e2_h[:, :],
                                    e2_gt[:, gi, ei:ei + 1],
                                    e2_comb[:, gi, :], ALU.mult, ALU.add),
                                deps=[hsl, hacc[gi], hgtL[gi]])
                            hacc[gi] = hac
                            st2["b03"] = hsl
                            st2["hfree"] = hac
                        tb = sc.barrier_handle("tensor")
                        st2["wfree"] = [[tb], [tb]]
                        st2["ebfree"] = [tb]
                    hpw = {}
                    for hf in range(2):
                        hpw[hf] = sc.op("sync", lambda e, hf=hf: e.dma_start(
                            e2_w[:, hf, :, :], pw_v2[hf]),
                            deps=st2["wfree"][hf], dma=True)
                    hpb = sc.op("gpsimd", lambda e: e.dma_start(
                        e2_eb[0:1, :], proj_b), deps=st2["ebfree"], dma=True)
                    for gi, m in enumerate(sg):
                        lt0 = m * 128
                        htr = sc.op("tensor", lambda e, gi=gi: (
                            [e.transpose(psb[:, 4 + kt // 8, (kt % 8) * 128:
                                             (kt % 8) * 128 + 128],
                                         e2_comb[:, gi, ts(kt)], sb_ident[:, :])
                             for kt in range(16)][-1]),
                            deps=[hacc[gi], st2["b45"]])
                        hev = sc.op("scalar", lambda e: e.copy(
                            e2_cT.rearrange("p a b -> p (a b)"),
                            psb[:, 4:6, :].rearrange("p a b -> p (a b)")),
                            deps=[htr, st2["cTfree"]])
                        st2["b45"] = hev
                        hpm = sc.op("tensor", lambda e: (
                            [[e.matmul(ps[:, sl, :], e2_cT[:, kt, :],
                                       e2_w[:, kt // 8, kt % 8, ts(sl, 512)],
                                       start=(kt == 0), stop=False,
                                       skip_group_check=True)
                              for kt in range(16)][-1] for sl in range(4)][-1],
                            [e.matmul(ps[:, sl, :], sb_onesr[:, :],
                                      e2_eb[0:1, ts(sl, 512)], start=False,
                                      stop=True, skip_group_check=True)
                             for sl in range(4)][-1])[-1],
                            deps=[hev, hpw[0], hpw[1], hpb, st2["b03"]])
                        st2["cTfree"] = hpm
                        hx2 = sc.op("sync", lambda e, lt0=lt0: e.dma_start(
                            e2_x[:, :], ln2s[lt0:lt0 + 128, :]),
                            deps=[st2["xfree"]], dma=True)
                        hx1L = sc.op("sync", lambda e, lt0=lt0: e.dma_start(
                            e2_x1[:, :], x1s[lt0:lt0 + 128, :]),
                            deps=[st2["x1free"]], dma=True)
                        hy = sc.op("vector", lambda e: e.tensor_tensor(
                            e2_y[:, :], pv03, e2_x[:, :], ALU.add),
                            deps=[hpm, hx2, st2["yfree"], st2["hfree"]])
                        st2["b03"] = hy
                        hq = sc.op("scalar", lambda e: e.activation(
                            e2_fr[:, :], e2_y[:, :], AF.Square,
                            accum_out=m2c(0)), deps=[hy] + st2["frfree"])
                        hms = sc.op("vector", lambda e: e.tensor_scalar(
                            m2c(1), m2c(0), 1.0 / D, None, ALU.mult), deps=[hq])
                        hrst = sc.op("scalar", lambda e: e.activation(
                            m2c(1), m2c(1), AF.Ln, bias=sb_eps[:, 1:2]),
                            deps=[hms])
                        hrst = sc.op("scalar", lambda e: e.activation(
                            m2c(1), m2c(1), AF.Exp, scale=-0.5))
                        hfr = sc.op("vector", lambda e: e.tensor_scalar(
                            e2_fr[:, :], e2_y[:, :], m2c(1), None, ALU.mult),
                            deps=[hrst])
                        hfr = sc.op("vector", lambda e: e.tensor_tensor(
                            e2_fr[:, :], e2_fr[:, :], e2_fnb[:, :], ALU.mult))
                        hs1 = sc.op("vector", lambda e: e.tensor_reduce(
                            m2c(2), e2_fr[:, :], AX.X, ALU.add))
                        hq2 = sc.op("scalar", lambda e: e.activation(
                            e2_y[:, :], e2_fr[:, :], AF.Square,
                            accum_out=m2c(3)), deps=[hfr])
                        hmn = sc.op("vector", lambda e: e.tensor_scalar(
                            m2c(4), m2c(2), 1.0 / D, None, ALU.mult), deps=[hs1])
                        hms2 = sc.op("vector", lambda e: e.tensor_scalar(
                            m2c(5), m2c(4), m2c(4), None, ALU.mult))
                        hv2 = sc.op("vector", lambda e: e.tensor_scalar(
                            m2c(6), m2c(3), 1.0 / D, None, ALU.mult), deps=[hq2])
                        hv2 = sc.op("vector", lambda e: e.tensor_tensor(
                            m2c(6), m2c(6), m2c(5), ALU.subtract))
                        hr2 = sc.op("scalar", lambda e: e.activation(
                            m2c(7), m2c(6), AF.Ln, bias=sb_eps[:, 0:1]),
                            deps=[hv2])
                        hr2 = sc.op("scalar", lambda e: e.activation(
                            m2c(7), m2c(7), AF.Exp, scale=-0.5))
                        ho = sc.op("vector", lambda e: e.tensor_scalar(
                            e2_y[:, :], e2_fr[:, :], m2c(4), m2c(7),
                            ALU.subtract, ALU.mult), deps=[hr2, hq2])
                        ho = sc.op("vector", lambda e: e.tensor_tensor(
                            e2_y[:, :], e2_y[:, :], e2_x1[:, :], ALU.add),
                            deps=[hx1L])
                        st2["frfree"] = [ho]
                        host = sc.op("sync", lambda e, lt0=lt0: e.dma_start(
                            out[lt0:lt0 + 128, :], e2_y[:, :]), deps=[ho],
                            dma=True)
                        st2["yfree"] = host
                        st2["xfree"] = hy
                        st2["x1free"] = ho
                    st2["l2Tfree"] = [sc.barrier_handle("tensor")]
                    st2["combfree"] = [sc.barrier_handle("tensor")]
                    st2["gtfree"] = [sc.barrier_handle("vector")]
                    st2["wfree"] = [[hpm], [hpm]]
                    st2["ebfree"] = [hpm]

            lnxT_free = None
            prev_banks = None
            fs = {}
            for ip, (tst, ntk, full) in enumerate(passes):
                if stage == "A" and ip > 0:
                    break
                hlast_mm, hsc_barrier, fs = front_pass(tst, ntk, full, lnxT_free,
                                                       prev_banks, fs)
                lnxT_free = hlast_mm
                prev_banks = hsc_barrier
                if ip == 0 and "lnxT" in dbg_outs:
                    aev = sc.barrier_handle("scalar")
                    o0 = tst - max(tst - 128, 0)
                    for kt in range(16):
                        sc.op("gpsimd", lambda e, kt=kt, o0=o0: e.dma_start(
                            dbg_outs["lnxT"][ts(kt), 0:1280], v_lnxT[:, kt, o0:o0 + 1280]),
                            deps=[aev], dma=True)

            if stage not in ("A", "front"):
                front_done = [sc.barrier_handle(e) for e in ENGS] + \
                             [sc.barrier_handle("sync", dma=True),
                              sc.barrier_handle("gpsimd", dma=True)]
                stS = scan_phase(front_done)
                if stage == "all" and int(os.environ.get("TAPSTOP", "0")) == 0:
                    scan_done = [sc.barrier_handle(e) for e in ENGS] + \
                                [sc.barrier_handle("sync", dma=True),
                                 sc.barrier_handle("gpsimd", dma=True)]
                    epilogue_phase(scan_done)

            front_all = [sc.barrier_handle(e) for e in ENGS] + \
                        [sc.barrier_handle("sync", dma=True),
                         sc.barrier_handle("gpsimd", dma=True)]

            def tap(name, emitters):
                if name not in dbg_outs:
                    return
                for fn in emitters:
                    sc.op("gpsimd", fn, deps=front_all, dma=True)

            tap("qs", [lambda e, hh=hh: e.dma_start(
                dbg_outs["qs"][ts(hh), :], qs_v[:, hh, 0:1280]) for hh in range(16)])
            tap("kfs", [lambda e, hh=hh: e.dma_start(
                dbg_outs["kfs"][ts(hh), :], kfs_v[:, hh, 0:1280]) for hh in range(16)])
            tap("zs", [lambda e, hh=hh: e.dma_start(
                dbg_outs["zs"][ts(hh), :], zs_v[:, hh, 0:1280]) for hh in range(32)])
            tap("kts", [lambda e, n=n: e.dma_start(
                dbg_outs["kts"][ts(n), :], kts_t[n]) for n in range(10)])
            tap("vts", [lambda e, n=n: e.dma_start(
                dbg_outs["vts"][ts(n), :], vts_t[n]) for n in range(10)])
            tap("bas", [lambda e: e.dma_start(dbg_outs["bas"][:, :], bas[:, 0:1280])])
            yo = 0 if role == 0 else P
            tap("ys", [lambda e, hh=hh, yo=yo: e.dma_start(
                dbg_outs["ys"][ts(hh), :], ys_v[:, hh, yo:yo + 1280]) for hh in range(32)])
            tap("x1r", [lambda e: e.dma_start(dbg_outs["x1r"], x1s[0:256, :])])
            tap("l2r", [lambda e: e.dma_start(dbg_outs["l2r"], ln2s[0:256, :])])
            tap("gt", [lambda e: e.dma_start(dbg_outs["gt"], gates[0:256, :])])
            tap("l2tr", [lambda e: e.dma_start(dbg_outs["l2tr"], l2ts[0:256, 0:256])])

            fin = [sc.barrier_handle(e) for e in ENGS] + \
                  [sc.barrier_handle(e, dma=True) for e in ("sync", "gpsimd", "scalar")]
            if (stage != "all" or int(os.environ.get("TAPSTOP", "0")) > 0
                    or os.environ.get("EPSTOP", "0") == "1"):
                hout = sc.op("sync", lambda e: e.dma_start(out[0:128, :], v_x[:, 0, :]),
                             deps=fin, dma=True)
                sc.op("sync", lambda e: e.nop(), deps=[hout])
            else:
                sc.op("sync", lambda e: e.nop(), deps=fin)
            return sc

        NOBRANCH = os.environ.get("NOBRANCH", "0") == "1"
        scripts = {0: build_script(0)}
        if not NOBRANCH:
            scripts[1] = build_script(1)

        def section(engname):
            def body(eng):
                if NOBRANCH:
                    scripts[0].replay(engname, eng)
                    return
                r = eng.to_reg(eng.partition_id() % 2)
                with eng.If_eq(r, 0):
                    scripts[0].replay(engname, eng)
                with eng.Else():
                    scripts[1].replay(engname, eng)
            return body

        block.sync(section("sync"))
        block.gpsimd(section("gpsimd"))
        block.tensor(section("tensor"))
        block.scalar(section("scalar"))
        block.vector(section("vector"))

    return nc

import numpy as np
from concourse.bass_utils import run_bass_kernel_spmd

_NC_CACHE = {}


def kernel(**inputs):
    x = np.asarray(inputs["L0_attn_hidden_states"], np.float32)
    conv_w = np.asarray(inputs["L0_attn_conv_w"], np.float32).reshape(8192, 4)
    expert_w = np.asarray(inputs["L0_ffn_expert_w"], np.float32).reshape(8 * 2048, 2048)
    base = {
        "w_qkvz": np.asarray(inputs["L0_attn_w_qkvz"], np.float32),
        "w_ba": np.asarray(inputs["L0_attn_w_ba"], np.float32),
        "conv_w": conv_w,
        "dt_bias": np.asarray(inputs["L0_attn_dt_bias"], np.float32).reshape(32, 1),
        "a_log": np.asarray(inputs["L0_attn_A_log"], np.float32).reshape(32, 1),
        "anorm_w": np.asarray(inputs["L0_attn_norm_w"], np.float32).reshape(128, 1),
        "w_out": np.asarray(inputs["L0_attn_w_out"], np.float32),
        "router_w": np.asarray(inputs["L0_ffn_router_w"], np.float32),
        "router_b": np.asarray(inputs["L0_ffn_router_b"], np.float32).reshape(1, 8),
        "expert_w": expert_w,
        "expert_b": np.asarray(inputs["L0_ffn_expert_b"], np.float32),
        "proj_w": np.asarray(inputs["L0_ffn_proj_w"], np.float32),
        "proj_b": np.asarray(inputs["L0_ffn_proj_b"], np.float32).reshape(1, 2048),
        "fnorm_w": np.asarray(inputs["L0_ffn_norm_w"], np.float32).reshape(1, 2048),
    }
    if "nc" not in _NC_CACHE:
        _NC_CACHE["nc"] = build(dbg=[])
    nc = _NC_CACHE["nc"]
    in_maps = []
    for r in range(8):
        m = dict(base)
        m["xb"] = np.ascontiguousarray(x[r // 2])
        in_maps.append(m)
    res = run_bass_kernel_spmd(nc, in_maps, core_ids=list(range(8)))
    full = np.empty((4, 4096, 2048), np.float32)
    for b in range(4):
        full[b, 0:T0] = res.results[2 * b]["out"]
        full[b, P:S] = res.results[2 * b + 1]["out"][0:T1]
    return full
